# version 21 (final)
# baseline (speedup 1.0000x reference)
import contextlib
import numpy as np

import concourse.bass as bass
import concourse.mybir as mybir
from concourse.bass_utils import run_bass_kernel_spmd

F32 = mybir.dt.float32
BF16 = mybir.dt.bfloat16
I32 = mybir.dt.int32
A = mybir.AluOpType
AF = mybir.ActivationFunctionType
AX = mybir.AxisListType.X

B, S, D, C, NI = 8, 4096, 1024, 256, 2816
NCORES = 8
RS = S // NCORES
NT = B * RS // 128
LR2 = 0.001 * 2.0 / float(S * D)
INV6 = 1.0 / 6.0
SIGNBIT = -2147483648


class Sem:
    def __init__(self, es, nc, name):
        self.h = es.enter_context(nc.semaphore(name))
        self.name = name
        self.v = 0

    def inc(self, inst, amt=1):
        inst.then_inc(self.h, amt)
        self.v += amt
        return self.v


REAL_INPUTS = {"x_shard", "wg_shard", "wu_shard", "wd_shard", "tkT", "tvT", "tqT",
               "W1T", "W2f", "W2T", "b1c", "b2c"}


class Prog:
    def __init__(self, name, fused=False):
        self.nc = bass.Bass("TRN2", target_bir_lowering=False)
        self.es = contextlib.ExitStack()
        self.cur_es = self.es
        self.name = name
        self.fused = fused
        self.outs = []
        self.tensors = {}
        self.all_sems = []
        self.block = None

    def get_block(self):
        if self.block is None:
            self.block = self.es.enter_context(self.nc.Block())
        return self.block

    def inp(self, name, shape, dt):
        if self.fused and name == "x_shard2":
            return self.tensors["x_shard"]
        if name in self.tensors:
            return self.tensors[name]
        kind = "ExternalInput" if (not self.fused or name in REAL_INPUTS) else "Internal"
        if kind == "ExternalInput":
            ap = self.nc.dram_tensor(name, list(shape), dt, kind="ExternalInput").ap()
        else:
            ap = self.nc.dram_tensor(name, list(shape), dt).ap()
        self.tensors[name] = ap
        return ap

    def out(self, name, shape, dt):
        if self.fused and name != "out_shard":
            if name in self.tensors:
                return self.tensors[name]
            ap = self.nc.dram_tensor(name, list(shape), dt).ap()
            self.tensors[name] = ap
            return ap
        self.outs.append(name)
        ap = self.nc.dram_tensor(name, list(shape), dt, kind="ExternalOutput").ap()
        self.tensors[name] = ap
        return ap

    def dram(self, name, shape, dt, addr_space="Local"):
        return self.nc.dram_tensor(name, list(shape), dt, addr_space=addr_space).ap()

    def sb(self, name, shape, dt):
        return self.cur_es.enter_context(self.nc.sbuf_tensor(name, list(shape), dt))

    def ps(self, name, shape):
        return self.cur_es.enter_context(self.nc.psum_tensor(name, list(shape), F32))

    def sem(self, name):
        s = Sem(self.es, self.nc, name)
        self.all_sems.append(s)
        return s


def emit_barrier(P, skip=()):
    block = P.get_block()
    sems = [s for s in P.all_sems if s.v > 0 and s.name not in skip]

    for engname in ["tensor", "vector", "scalar", "gpsimd", "sync"]:
        @block.__getattribute__(engname)
        def _(e):
            for s in sems:
                e.wait_ge(s.h, s.v)


def emit_quant_ladder(v, sv, ss_val, ss, scr, pdim, W, out_bf_ap, bitwise=True):
    nb = W // 16
    p = pdim
    an, qa, qb, sn = scr["an"], scr["qa"], scr["qb"], scr["sn"]

    def tv(i):
        v.wait_ge(sv.h, sv.inc(i))

    v.wait_ge(ss.h, ss_val)
    tv(v.tensor_scalar(qa[0:p, 0:W], an[0:p, 0:W], 0.25, None, A.is_gt))
    tv(v.scalar_tensor_tensor(qa[0:p, 0:W], an[0:p, 0:W], 0.75, qa[0:p, 0:W], A.is_gt, A.add))
    tv(v.scalar_tensor_tensor(qa[0:p, 0:W], an[0:p, 0:W], 1.25, qa[0:p, 0:W], A.is_gt, A.add))
    tv(v.scalar_tensor_tensor(qa[0:p, 0:W], an[0:p, 0:W], 1.75, qa[0:p, 0:W], A.is_gt, A.add))
    tv(v.tensor_scalar(qb[0:p, 0:W], an[0:p, 0:W], 5.0, 2.0, A.is_gt, A.mult))
    tv(v.scalar_tensor_tensor(qb[0:p, 0:W], an[0:p, 0:W], 2.5, qb[0:p, 0:W], A.is_gt, A.add))
    tv(v.scalar_tensor_tensor(qb[0:p, 0:W], an[0:p, 0:W], 3.5, qb[0:p, 0:W], A.is_gt, A.add))
    tv(v.scalar_tensor_tensor(qb[0:p, 0:W], qa[0:p, 0:W], 0.5, qb[0:p, 0:W], A.mult, A.add))
    sbb = scr["sbs"][0:p, 0:nb, None].broadcast_to([p, nb, 16])
    tv(v.tensor_tensor(an[0:p, 0:W].rearrange("p (nb b) -> p nb b", b=16),
                       qb[0:p, 0:W].rearrange("p (nb b) -> p nb b", b=16), sbb, A.mult))
    if bitwise:
        tv(v.tensor_scalar(qa[0:p, 0:W].bitcast(I32), sn[0:p, 0:W].bitcast(I32),
                           SIGNBIT, None, A.bitwise_and))
        tv(v.tensor_tensor(an[0:p, 0:W].bitcast(I32), an[0:p, 0:W].bitcast(I32),
                           qa[0:p, 0:W].bitcast(I32), A.bitwise_or))
    else:
        tv(v.tensor_scalar(qa[0:p, 0:W], sn[0:p, 0:W], 0.0, 2.0, A.is_ge, A.mult))
        tv(v.tensor_scalar(qa[0:p, 0:W], qa[0:p, 0:W], 1.0, None, A.subtract))
        tv(v.tensor_tensor(an[0:p, 0:W], an[0:p, 0:W], qa[0:p, 0:W], A.mult))
    tv(v.tensor_copy(out_bf_ap, an[0:p, 0:W]))


def emit_quant_pre(v, sv, src_f32, scr, pdim, W):
    emit_quant_scales(v, sv, src_f32, scr, pdim, W)
    emit_quant_sn(v, sv, src_f32, scr, pdim, W)


def emit_quant_scales(v, sv, src_f32, scr, pdim, W):
    nb = W // 16
    p = pdim
    x3 = src_f32[0:p, 0:W].rearrange("p (nb b) -> p nb b", b=16)

    def tv(i):
        v.wait_ge(sv.h, sv.inc(i))

    tv(v.tensor_reduce(scr["m"][0:p, 0:nb], x3, axis=AX, op=A.max,
                       apply_absolute_value=True))
    tv(v.tensor_scalar(scr["sbs"][0:p, 0:nb], scr["m"][0:p, 0:nb], INV6, 1e-12,
                       A.mult, A.max))
    tv(v.reciprocal(scr["rs"][0:p, 0:nb], scr["sbs"][0:p, 0:nb]))


def emit_quant_sn(v, sv, src_f32, scr, pdim, W):
    nb = W // 16
    p = pdim
    x3 = src_f32[0:p, 0:W].rearrange("p (nb b) -> p nb b", b=16)
    rsb = scr["rs"][0:p, 0:nb, None].broadcast_to([p, nb, 16])
    v.tensor_tensor(scr["sn"][0:p, 0:W].rearrange("p (nb b) -> p nb b", b=16),
                    x3, rsb, A.mult).then_inc(sv.h, 1)
    sv.v += 1
    v.wait_ge(sv.h, sv.v)


def quant_scratch3(P, W, tag=""):
    scr = {}
    for k in ["sn", "an"]:
        scr[k] = P.sb(f"q{tag}_{k}", [128, W], F32)
    for k in ["rA", "rB", "qv", "sgn"]:
        scr[k] = P.sb(f"q{tag}_{k}", [128, W], BF16)
    for k in ["m", "sbs", "rs", "sb2"]:
        scr[k] = P.sb(f"q{tag}_{k}", [128, W // 16], F32)
    return scr


def q3_pre(block, sems, scr, src_f32, W, pdim=128, extra_v_wait=None):
    sv, ss, gv = sems["sv"], sems["ss"], sems["gv"]
    nb = W // 16
    p = pdim
    sn, an, sgn = scr["sn"], scr["an"], scr["sgn"]
    war_gv = scr.get("war_gv", 0)

    @block.vector
    def _(v):
        def tv(i):
            v.wait_ge(sv.h, sv.inc(i))
        if extra_v_wait is not None:
            extra_v_wait(v)
        v.wait_ge(gv.h, war_gv)
        x3 = src_f32[0:p, 0:W].rearrange("p (n b) -> p n b", b=16)
        tv(v.tensor_reduce(scr["m"][0:p, 0:nb], x3, axis=AX, op=A.max,
                           apply_absolute_value=True))
        tv(v.tensor_scalar(scr["sbs"][0:p, 0:nb], scr["m"][0:p, 0:nb], INV6, 1e-12,
                           A.mult, A.max))
        tv(v.reciprocal(scr["rs"][0:p, 0:nb], scr["sbs"][0:p, 0:nb]))
        tv(v.tensor_scalar(scr["sb2"][0:p, 0:nb], scr["sbs"][0:p, 0:nb], 0.5, None,
                           A.mult))
    scales_tok = sv.v

    @block.gpsimd
    def _(g):
        g.wait_ge(sv.h, scales_tok)
        g.wait_ge(ss.h, scr.get("war_ss", 0))
        x3 = src_f32[0:p, 0:W].rearrange("p (n b) -> p n b", b=16)
        rsb = scr["rs"][0:p, 0:nb, None].broadcast_to([p, nb, 16])
        gv.inc(g.tensor_tensor(sn[0:p, 0:W].rearrange("p (n b) -> p n b", b=16),
                               x3, rsb, A.mult))
    sn_tok = gv.v
    scr["sn_done_gv"] = gv.v

    @block.scalar
    def _(s):
        s.wait_ge(gv.h, sn_tok)
        s.wait_ge(sv.h, scr.get("war_sv", 0))
        ss.inc(s.activation(an[0:p, 0:W], sn[0:p, 0:W], AF.Abs))
        ss.inc(s.activation(sgn[0:p, 0:W], sn[0:p, 0:W], AF.Sign))
    scr["abs_ss"] = ss.v
    return ss.v


def q3_ladder(block, sems, scr, out_bf, W, pdim=128, out_guard=None):
    sv, ss, gv = sems["sv"], sems["ss"], sems["gv"]
    nb = W // 16
    p = pdim
    an, rA, rB, qv, sgn = scr["an"], scr["rA"], scr["rB"], scr["qv"], scr["sgn"]
    abs_tok = scr["abs_ss"]

    @block.vector
    def _(v):
        def tv(i):
            v.wait_ge(sv.h, sv.inc(i))
        v.wait_ge(ss.h, abs_tok)
        v.wait_ge(gv.h, scr.get("war2_gv", 0))
        tv(v.tensor_scalar(rA[0:p, 0:W], an[0:p, 0:W], 0.25, None, A.is_gt))
        tv(v.tensor_scalar(rB[0:p, 0:W], an[0:p, 0:W], 0.75, None, A.is_gt))
        tv(v.tensor_tensor(rA[0:p, 0:W], rA[0:p, 0:W], rB[0:p, 0:W], A.add))
        tv(v.tensor_scalar(rB[0:p, 0:W], an[0:p, 0:W], 1.25, None, A.is_gt))
        tv(v.tensor_tensor(rA[0:p, 0:W], rA[0:p, 0:W], rB[0:p, 0:W], A.add))
        tv(v.tensor_scalar(rB[0:p, 0:W], an[0:p, 0:W], 1.75, None, A.is_gt))
        tv(v.tensor_tensor(rA[0:p, 0:W], rA[0:p, 0:W], rB[0:p, 0:W], A.add))
        tv(v.tensor_scalar(rB[0:p, 0:W], an[0:p, 0:W], 2.5, 2.0, A.is_gt, A.mult))
        tv(v.tensor_tensor(rA[0:p, 0:W], rA[0:p, 0:W], rB[0:p, 0:W], A.add))
        tv(v.tensor_scalar(rB[0:p, 0:W], an[0:p, 0:W], 3.5, 2.0, A.is_gt, A.mult))
        tv(v.tensor_tensor(rA[0:p, 0:W], rA[0:p, 0:W], rB[0:p, 0:W], A.add))
        tv(v.tensor_scalar(rB[0:p, 0:W], an[0:p, 0:W], 5.0, 4.0, A.is_gt, A.mult))
        tv(v.tensor_tensor(rA[0:p, 0:W], rA[0:p, 0:W], rB[0:p, 0:W], A.add))
    scr["war_sv"] = sv.v
    ladder_tok = sv.v

    @block.gpsimd
    def _(g):
        g.wait_ge(sv.h, ladder_tok)
        sbb = scr["sb2"][0:p, 0:nb, None].broadcast_to([p, nb, 16])
        gv.inc(g.tensor_tensor(qv[0:p, 0:W].rearrange("p (n b) -> p n b", b=16),
                               rA[0:p, 0:W].rearrange("p (n b) -> p n b", b=16),
                               sbb, A.mult))
        g.wait_ge(gv.h, gv.v)
        if out_guard is not None:
            out_guard(g)
        gv.inc(g.tensor_tensor(out_bf, qv[0:p, 0:W], sgn[0:p, 0:W], A.mult))
    scr["war_gv"] = gv.v - 1
    scr["war2_gv"] = gv.v
    scr["war_ss"] = scr["abs_ss"]
    return gv.v


def emit_quant3(block, sems, scr, src_f32, out_bf, W, extra_v_wait=None,
                extra_g_wait=None, out_guard=None, pdim=128):
    q3_pre(block, sems, scr, src_f32, W, pdim=pdim, extra_v_wait=extra_v_wait)
    return q3_ladder(block, sems, scr, out_bf, W, pdim=pdim, out_guard=out_guard)


def quant_scratch(P, W, tag=""):
    scr = {k: P.sb(f"q{tag}_{k}", [128, W], F32) for k in ["sn", "an", "qa", "qb"]}
    scr["m"] = P.sb(f"q{tag}_m", [128, W // 16], F32)
    scr["sbs"] = P.sb(f"q{tag}_sbs", [128, W // 16], F32)
    scr["rs"] = P.sb(f"q{tag}_rs", [128, W // 16], F32)
    return scr



def build_s1(fused=None):
    P = fused or Prog("s1")
    nc = P.nc
    x = P.inp("x_shard", (B * RS, D), F32)
    xn_hbm = P.out("xn_bf", (B * RS, D), BF16)
    xq_hbm = P.out("xq_bf", (B * RS, D), BF16)

    xa = [P.sb(f"s1_xa{i}", [128, D], F32) for i in range(2)]
    dif = [P.sb(f"s1_dif{i}", [128, D], F32) for i in range(2)]
    xnt = [P.sb(f"s1_xnt{i}", [128, D], F32) for i in range(2)]
    xnb = [P.sb(f"s1_xnb{i}", [128, D], BF16) for i in range(2)]
    xqb = [P.sb(f"s1_xqb{i}", [128, D], BF16) for i in range(2)]
    scrx = [quant_scratch3(P, D, f"s1q{i}") for i in range(2)]
    cols = P.sb("s1_cols", [128, 8], F32)

    ld = P.sem("s1_ld")
    sv = P.sem("s1_sv")
    ss = P.sem("s1_ss")
    gd = P.sem("s1_gd")
    gv = P.sem("s1_gv")
    qsems = {"sv": sv, "ss": ss, "gv": gv}

    xa_free = {}
    xn_spill = {}
    xq_spill = {}
    tok = {}
    block = P.get_block()

    def emit_ladder_and_spill(t):
        par = t % 2

        def out_guard(g, t=t):
            if t >= 2:
                g.wait_ge(gd.h, xq_spill[t - 2])
        q3_ladder(block, qsems, scrx[t % 2], xqb[par][:], D, out_guard=out_guard)

        @block.gpsimd
        def _(g, t=t, par=par):
            g.wait_ge(gv.h, gv.v)
            g.wait_ge(gd.h, gd.v)
            g.dma_start(out=xq_hbm[128 * t:128 * (t + 1), :], in_=xqb[par][:])\
             .then_inc(gd.h, 16)
        gd.v += 16
        xq_spill[t] = gd.v

    cols2 = [cols, P.sb("s1_cols2", [128, 8], F32)]
    sqjunk = P.sb("s1_sqjunk", [128, D], F32)

    def emit_load(t):
        par = t % 2

        @block.sync
        def _(sy, t=t, par=par):
            if t >= 2:
                sy.wait_ge(sv.h, xa_free[t - 2])
            sy.wait_ge(ld.h, ld.v)
            sy.dma_start(out=xa[par][:], in_=x[128 * t:128 * (t + 1), :])\
              .then_inc(ld.h, 16)
        ld.v += 16
        ld_tok[t] = ld.v

    def emit_ln_a(t):
        par = t % 2
        cl = cols2[t % 2]

        @block.vector
        def _(v, t=t, par=par, cl=cl):
            def tv(i):
                v.wait_ge(sv.h, sv.inc(i))
            v.wait_ge(ld.h, ld_tok[t])
            if t >= 2:
                v.wait_ge(ss.h, sq_tok.get(t - 2, 0))
            tv(v.reduce_sum(cl[:, 0:1], xa[par][:], axis=AX))
            tv(v.tensor_scalar(cl[:, 1:2], cl[:, 0:1], 1.0 / D, None, A.mult))
            tv(v.tensor_scalar(dif[par][:], xa[par][:], cl[:, 1:2], None, A.subtract))
            xa_free[t] = sv.v
        dif_tok[t] = sv.v

        @block.scalar
        def _(s, t=t, par=par, cl=cl):
            s.wait_ge(sv.h, dif_tok[t])
            ss.inc(s.activation(sqjunk[:], dif[par][:], AF.Square,
                                accum_out=cl[:, 2:3]))
        sq_tok[t] = ss.v

    def emit_ln_b(t):
        par = t % 2
        cl = cols2[t % 2]

        @block.vector
        def _(v, t=t, par=par, cl=cl):
            def tv(i):
                v.wait_ge(sv.h, sv.inc(i))
            v.wait_ge(ss.h, sq_tok[t])
            tv(v.tensor_scalar(cl[:, 3:4], cl[:, 2:3], 1.0 / D, 1e-5, A.mult, A.add))
            tv(v.reciprocal(cl[:, 4:5], cl[:, 3:4]))
        rv_tok[t] = sv.v

        @block.scalar
        def _(s, t=t, cl=cl):
            s.wait_ge(sv.h, rv_tok[t])
            ss.inc(s.activation(cl[:, 5:6], cl[:, 4:5], AF.Sqrt))
        sqrt_tok[t] = ss.v

        @block.vector
        def _(v, t=t, par=par, cl=cl):
            def tv(i):
                v.wait_ge(sv.h, sv.inc(i))
            v.wait_ge(ss.h, sqrt_tok[t])
            if t >= 2:
                v.wait_ge(gv.h, scrx[t % 2].get("sn_done_gv", 0))
            tv(v.tensor_scalar(xnt[par][:], dif[par][:], cl[:, 5:6], None, A.mult))
            if t >= 2:
                v.wait_ge(gd.h, xn_spill[t - 2])
            tv(v.tensor_copy(xnb[par][:], xnt[par][:]))

        @block.gpsimd
        def _(g, t=t, par=par):
            g.wait_ge(sv.h, sv.v)
            g.wait_ge(gd.h, gd.v)
            g.dma_start(out=xn_hbm[128 * t:128 * (t + 1), :], in_=xnb[par][:])\
             .then_inc(gd.h, 16)
        gd.v += 16
        xn_spill[t] = gd.v

    ld_tok = {}
    dif_tok = {}
    sq_tok = {}
    rv_tok = {}
    sqrt_tok = {}
    emit_load(0)
    emit_ln_a(0)
    for t in range(NT):
        if t + 1 < NT:
            emit_load(t + 1)
            emit_ln_a(t + 1)
        emit_ln_b(t)
        q3_pre(block, qsems, scrx[t % 2], xnt[t % 2][:], D)
        if t >= 1:
            emit_ladder_and_spill(t - 1)

    emit_ladder_and_spill(NT - 1)

    @block.gpsimd
    def _(g):
        g.wait_ge(gd.h, gd.v)

    return P


def host_inputs_s1(inputs, core):
    x = np.ascontiguousarray(
        inputs["L0_attn_in_seq"][:, RS * core:RS * (core + 1), :]).reshape(B * RS, D)
    return {"x_shard": x}


def build_fused():
    P = Prog("mega", fused=True)
    P.get_block()
    for gi, group in enumerate(((build_s1, build_s2), (build_s3,), (build_s4,),
                                (build_s5,))):
        st = contextlib.ExitStack()
        P.cur_es = st
        for builder in group:
            builder(P)
        emit_barrier(P, skip=("s2_cc",) if gi == 0 else ())
        st.close()
    P.cur_es = P.es
    return P


_PROG = None


def _get_prog():
    global _PROG
    if _PROG is None:
        _PROG = build_fused()
    return _PROG


def host_core_inputs(inputs, c):
    m = host_inputs_s1(inputs, c)
    m.update(host_inputs_s2(inputs, c))
    m.update({
        "W1T": np.ascontiguousarray(inputs["L0_attn_W1"].T),
        "W2f": np.ascontiguousarray(inputs["L0_attn_W2"]),
        "W2T": np.ascontiguousarray(inputs["L0_attn_W2"].T),
        "b1c": np.ascontiguousarray(np.asarray(inputs["L0_attn_b1"]).reshape(2, 128).T),
        "b2c": np.ascontiguousarray(np.asarray(inputs["L0_attn_b2"]).reshape(8, 128).T),
    })
    return m


def kernel(**inputs):
    P = _get_prog()
    cores = list(range(NCORES))
    r = run_bass_kernel_spmd(P.nc, [host_core_inputs(inputs, c) for c in cores],
                             core_ids=cores)
    out = np.empty((B, S, D), np.float32)
    for c in cores:
        out[:, RS * c:RS * (c + 1), :] = r.results[c]["out_shard"].reshape(B, RS, D)
    return out



WG_SH = NI // NCORES
WD_SH = D // NCORES


def build_s2(fused=None):
    P = fused or Prog("s2")
    nc = P.nc
    wg_sh = P.inp("wg_shard", (WG_SH, D), F32)
    wu_sh = P.inp("wu_shard", (WG_SH, D), F32)
    wd_sh = P.inp("wd_shard", (WD_SH, NI), F32)
    tkT = P.inp("tkT", (D, D), F32)
    tvT = P.inp("tvT", (D, D), F32)
    tqT = P.inp("tqT", (D, D), F32)

    wqg_full = P.out("wqg_full", (NI, D), BF16)
    wqu_full = P.out("wqu_full", (NI, D), BF16)
    wqd_full = P.out("wqd_full", (D, NI), BF16)
    tkTb = P.out("tkTb", (D, D), BF16)
    tvTb = P.out("tvTb", (D, D), BF16)
    tqTb = P.out("tqTb", (D, D), BF16)

    g_self = P.dram("s2_g_self", (WG_SH, D), BF16)
    u_self = P.dram("s2_u_self", (WG_SH, D), BF16)
    d_self = P.dram("s2_d_self", (WD_SH, NI), BF16)
    if P.fused:
        g_all, u_all, d_all = wqg_full, wqu_full, wqd_full
    else:
        g_all = P.dram("s2_g_all", (NI, D), BF16, addr_space="Shared")
        u_all = P.dram("s2_u_all", (NI, D), BF16, addr_space="Shared")
        d_all = P.dram("s2_d_all", (D, NI), BF16, addr_space="Shared")

    wa = [P.sb(f"s2_wa{i}", [128, NI], F32) for i in range(2)]
    wb = [P.sb(f"s2_wb{i}", [128, NI], BF16) for i in range(2)]
    scr2 = [quant_scratch3(P, NI, f"s2q{i}") for i in range(2)]

    ld = P.sem("s2_ld")
    sv = P.sem("s2_sv")
    ss = P.sem("s2_ss")
    gd = P.sem("s2_gd")
    gv = P.sem("s2_gv")
    cc = P.sem("s2_cc")
    qsems = {"sv": sv, "ss": ss, "gv": gv}

    block = P.get_block()

    tiles = []
    for srct, dst, rows, W in [(wg_sh, g_self, WG_SH, D), (wu_sh, u_self, WG_SH, D),
                               (wd_sh, d_self, WD_SH, NI)]:
        r0 = 0
        while r0 < rows:
            p = min(128, rows - r0)
            tiles.append((srct, dst, r0, p, W))
            r0 += p

    ld_tok = {}
    spill_tok = {}

    def emit_load(k):
        srct, dst, r0, p, W = tiles[k]

        @block.sync
        def _(sy, k=k, srct=srct, r0=r0, p=p, W=W):
            if k >= 2:
                sy.wait_ge(gv.h, scr2[k % 2].get("sn_done_gv", 0))
                sy.wait_ge(sv.h, sv.v)
            sy.wait_ge(ld.h, ld.v)
            sy.dma_start(out=wa[k % 2][0:p, 0:W], in_=srct[r0:r0 + p, :])\
              .then_inc(ld.h, 16)
        ld.v += 16
        ld_tok[k] = ld.v

    def emit_quant_k(k):
        srct, dst, r0, p, W = tiles[k]

        def ewv(v, k=k):
            v.wait_ge(ld.h, ld_tok[k])
        q3_pre(block, qsems, scr2[k % 2], wa[k % 2][0:p, 0:W], W, pdim=p,
               extra_v_wait=ewv)

    def emit_ladder_k(k):
        srct, dst, r0, p, W = tiles[k]

        def og(g, k=k):
            if k >= 2:
                g.wait_ge(gd.h, spill_tok[k - 2])
        q3_ladder(block, qsems, scr2[k % 2], wb[k % 2][0:p, 0:W], W, pdim=p,
                  out_guard=og)

        @block.gpsimd
        def _(g, k=k, dst=dst, r0=r0, p=p, W=W):
            g.wait_ge(gv.h, gv.v)
            g.wait_ge(gd.h, gd.v)
            g.dma_start(out=dst[r0:r0 + p, :], in_=wb[k % 2][0:p, 0:W]).then_inc(gd.h, 16)
        gd.v += 16
        spill_tok[k] = gd.v

    emit_load(0)
    for k in range(len(tiles)):
        if k + 1 < len(tiles):
            emit_load(k + 1)
        emit_quant_k(k)
        if k >= 1:
            emit_ladder_k(k - 1)
    emit_ladder_k(len(tiles) - 1)

    for src, dst in [(tkT, tkTb), (tvT, tvTb), (tqT, tqTb)]:
        for tchunk in range(D // 128):
            @block.sync
            def _(sy, src=src, tchunk=tchunk):
                sy.wait_ge(gd.h, gd.v)
                sy.wait_ge(ld.h, ld.v)
                sy.dma_start(out=wa[0][:, 0:D], in_=src[128 * tchunk:128 * (tchunk + 1), :])\
                  .then_inc(ld.h, 16)
            ld.v += 16
            ld_tok = ld.v

            @block.vector
            def _(v, ld_tok=ld_tok):
                v.wait_ge(ld.h, ld_tok)
                v.wait_ge(gd.h, gd.v)
                sv.inc(v.tensor_copy(wb[0][:, 0:D], wa[0][:, 0:D]))
                v.wait_ge(sv.h, sv.v)

            @block.gpsimd
            def _(g, dst=dst, tchunk=tchunk):
                g.wait_ge(sv.h, sv.v)
                g.wait_ge(gd.h, gd.v)
                g.dma_start(out=dst[128 * tchunk:128 * (tchunk + 1), :], in_=wb[0][:, 0:D])\
                 .then_inc(gd.h, 16)
            gd.v += 16

    @block.gpsimd
    def _(g):
        g.wait_ge(gd.h, gd.v)
        for self_t, all_t in [(g_self, g_all), (u_self, u_all), (d_self, d_all)]:
            cc.inc(g.collective_compute(
                "AllGather", A.bypass,
                replica_groups=[list(range(NCORES))],
                ins=[self_t.flatten()[None, :].opt()],
                outs=[all_t.flatten()[None, :].opt()],
            ))
        if not P.fused:
            g.wait_ge(cc.h, cc.v)
            base = gd.v
            for i, (all_t, ext) in enumerate([(g_all, wqg_full), (u_all, wqu_full),
                                              (d_all, wqd_full)]):
                g.wait_ge(gd.h, base + 16 * i)
                g.dma_start(out=ext[:, :], in_=all_t[:, :]).then_inc(gd.h, 16)
            gd.v += 48

    @block.gpsimd
    def _(g):
        g.wait_ge(gd.h, gd.v)

    return P


def host_inputs_s2(inputs, core):
    wg = np.ascontiguousarray(inputs["L0_ffn_w_gate"][WG_SH * core:WG_SH * (core + 1)])
    wu = np.ascontiguousarray(inputs["L0_ffn_w_up"][WG_SH * core:WG_SH * (core + 1)])
    wd = np.ascontiguousarray(inputs["L0_ffn_w_down"][WD_SH * core:WD_SH * (core + 1)])
    return {
        "wg_shard": wg, "wu_shard": wu, "wd_shard": wd,
        "tkT": np.ascontiguousarray(inputs["L0_attn_t_k"].T),
        "tvT": np.ascontiguousarray(inputs["L0_attn_t_v"].T),
        "tqT": np.ascontiguousarray(inputs["L0_attn_t_q"].T),
    }



GW = 2048
ARW = 3 * GW + 2 + 8


def build_s3(fused=None):
    P = fused or Prog("s3")
    nc = P.nc
    e = {}
    e["xn_hbm"] = P.inp("xn_bf", (B * RS, D), BF16)
    tk_h = P.inp("tkTb", (D, D), BF16)
    tv_h = P.inp("tvTb", (D, D), BF16)
    tq_h = P.inp("tqTb", (D, D), BF16)
    w1t_h = P.inp("W1T", (D, C), F32)
    w2_h = P.inp("W2f", (D, C), F32)
    w2t_h = P.inp("W2T", (C, D), F32)
    b1_h = P.inp("b1c", (128, 2), F32)
    b2_h = P.inp("b2c", (128, 8), F32)
    e["s_hbm"] = P.out("s_f32", (B * RS, D), F32)

    e["zb_h"] = P.dram("s3_zb", (D, RS), BF16)
    e["hb_h"] = P.dram("s3_hb", (C, RS), BF16)
    e["drb_h"] = P.dram("s3_drb", (D, RS), BF16)
    e["dhb_h"] = P.dram("s3_dhb", (C, RS), BF16)
    e["b2r_h"] = P.dram("s3_b2r", (D,), F32)
    e["ar_in"] = P.dram("s3_ar_in", (128, ARW), BF16)
    e["ar_out"] = P.dram("s3_ar_out", (128, ARW), BF16, addr_space="Shared")

    e["tk_sb"] = P.sb("s3_tk", [128, 8 * D], BF16)
    e["tv_sb"] = P.sb("s3_tv", [128, 8 * D], BF16)
    e["tq_sb"] = P.sb("s3_tq", [128, 8 * D], BF16)
    e["w1t_m"] = P.sb("s3_w1t_m", [128, GW], F32)
    e["w2_m"] = P.sb("s3_w2_m", [128, GW], F32)
    e["w2t_m"] = P.sb("s3_w2t_m", [128, GW], F32)
    e["w1t_b"] = P.sb("s3_w1t_b", [128, GW], BF16)
    e["w2_b"] = P.sb("s3_w2_b", [128, GW], BF16)
    e["w2t_b"] = P.sb("s3_w2t_b", [128, GW], BF16)
    e["b1_m"] = P.sb("s3_b1_m", [128, 2], F32)
    e["b2_m"] = P.sb("s3_b2_m", [128, 8], F32)
    e["b2row"] = P.sb("s3_b2row", [1, D], F32)
    e["ones1"] = P.sb("s3_ones1", [1, 128], F32)
    e["xn_fm"] = [P.sb(f"s3_xn_fm{i}", [128, 8 * RS], BF16) for i in range(2)]
    e["q_bf"] = [P.sb(f"s3_q_bf{i}", [128, 8 * RS], BF16) for i in range(2)]
    e["z_bf"] = P.sb("s3_z_bf", [128, 8 * RS], BF16)
    e["zy_bf"] = P.sb("s3_zy_bf", [128, 8 * RS], BF16)
    e["y_sb"] = P.sb("s3_y_sb", [128, RS], F32)
    e["hpre"] = P.sb("s3_hpre", [128, 2 * RS], F32)
    e["h_f"] = P.sb("s3_h_f", [128, 2 * RS], F32)
    e["h_bf"] = P.sb("s3_h_bf", [128, 2 * RS], BF16)
    e["dr_bf"] = P.sb("s3_dr_bf", [128, 8 * RS], BF16)
    e["dh_bf"] = P.sb("s3_dh_bf", [128, 2 * RS], BF16)
    e["h2_bf"] = P.sb("s3_h2_bf", [128, 2 * RS], BF16)
    e["z_tm"] = P.sb("s3_z_tm", [128, 4 * D], BF16)
    e["dr_tm"] = P.sb("s3_dr_tm", [128, 4 * D], BF16)
    e["h_tm"] = P.sb("s3_h_tm", [128, 4 * C], BF16)
    e["dh_tm"] = P.sb("s3_dh_tm", [128, 4 * C], BF16)
    pool = P.sb("s3_pool", [128, 3 * D], F32)
    e["scrA"] = pool[:, 0:D]
    e["scrB"] = pool[:, D:2 * D]
    e["scrC"] = pool[:, 2 * D:3 * D]
    e["s_sb"] = pool[:, 0:2 * D]
    e["gsend"] = P.sb("s3_gsend", [128, ARW], BF16)

    e["pA"] = P.ps("s3_pA", [128, 512])
    e["pB"] = P.ps("s3_pB", [128, 512])
    e["pC"] = P.ps("s3_pC", [128, 512])
    e["pD"] = P.ps("s3_pD", [128, 512])
    e["pG"] = P.ps("s3_pG", [128, GW])

    for s in ["ld", "ts", "sp", "sv", "ss", "pe", "cc"]:
        e[s] = P.sem("s3_" + s)
    (ld, ts, sp, sv, ss, pe, cc) = (e[k] for k in ["ld", "ts", "sp", "sv", "ss", "pe", "cc"])

    block = P.get_block()
    e["block"] = block
    e["nc"] = nc
    e["tok"] = {}

    @block.sync
    def _(sy):
        k = [0]
        def gate():
            sy.wait_ge(ld.h, ld.v + 16 * k[0])
            k[0] += 1
        for dst, srch in [(e["tk_sb"], tk_h), (e["tv_sb"], tv_h), (e["tq_sb"], tq_h)]:
            gate()
            sy.dma_start(out=dst.rearrange("p (dc j) -> p dc j", dc=8),
                         in_=srch.rearrange("(dc p) j -> p dc j", p=128))\
              .then_inc(ld.h, 16)
        gate()
        sy.dma_start(out=e["w1t_m"].rearrange("p (dc c) -> p dc c", dc=8),
                     in_=w1t_h.rearrange("(dc p) c -> p dc c", p=128)).then_inc(ld.h, 16)
        gate()
        sy.dma_start(out=e["w2_m"].rearrange("p (dc c) -> p dc c", dc=8),
                     in_=w2_h.rearrange("(dc p) c -> p dc c", p=128)).then_inc(ld.h, 16)
        gate()
        sy.dma_start(out=e["w2t_m"].rearrange("p (cc d) -> p cc d", cc=2),
                     in_=w2t_h.rearrange("(cc p) d -> p cc d", p=128)).then_inc(ld.h, 16)
        gate()
        sy.dma_start(out=e["b1_m"][:], in_=b1_h[:, :]).then_inc(ld.h, 16)
        gate()
        sy.dma_start(out=e["b2_m"][:], in_=b2_h[:, :]).then_inc(ld.h, 16)
    ld.v += 8 * 16
    e["pre_ld"] = ld.v

    @block.vector
    def _(v):
        def tv(i):
            v.wait_ge(sv.h, sv.inc(i))
        v.wait_ge(ld.h, e["pre_ld"])
        tv(v.memset(e["ones1"][:], 1.0))
        tv(v.tensor_copy(e["w1t_b"][:], e["w1t_m"][:]))
        tv(v.tensor_copy(e["w2_b"][:], e["w2_m"][:]))
        tv(v.tensor_copy(e["w2t_b"][:], e["w2t_m"][:]))
    e["pre_sv"] = sv.v

    @block.sync
    def _(sy):
        sy.wait_ge(ld.h, e["pre_ld"])
        sy.dma_start(out=e["b2r_h"].rearrange("(dc p) -> p dc", p=128), in_=e["b2_m"][:],
                     allow_slow_non_contiguous=True).then_inc(ld.h, 16)
        sy.wait_ge(ld.h, ld.v + 16)
        sy.dma_start(out=e["b2row"][0:1, :], in_=e["b2r_h"][None, :]).then_inc(ld.h, 16)
    ld.v += 32

    emit_front(e, 0)
    for t in range(B):
        emit_step(e, t)

    @block.gpsimd
    def _(g):
        g.wait_ge(sp.h, sp.v)

    return P


def emit_front(e, t):
    block, tok = e["block"], e["tok"]
    (ld, ts, sp, sv, ss, pe, cc) = (e[k] for k in ["ld", "ts", "sp", "sv", "ss", "pe", "cc"])
    xn_fm = e["xn_fm"][t % 2]
    q_bf = e["q_bf"][t % 2]
    (z_bf, zy_bf, y_sb) = (e["z_bf"], e["zy_bf"], e["y_sb"])
    (pA, pB, pC, pD) = (e["pA"], e["pB"], e["pC"], e["pD"])
    r0 = RS * t

    @block.scalar
    def _(s, t=t):
        if t >= 2:
            s.wait_ge(pe.h, tok[(t - 2, "xn_consumed")])
        s.wait_ge(ts.h, ts.v)
        s.dma_start_transpose(xn_fm.rearrange("p (dc r) -> p dc r", dc=8),
                              e["xn_hbm"][r0:r0 + RS, :]).then_inc(ts.h, 16)
    ts.v += 16
    tok[(t, "xn_fm")] = ts.v

    for jc in range(8):
        pY = pA if jc % 2 == 0 else pC
        pZ = pB if jc % 2 == 0 else pD

        @block.tensor
        def _(te, jc=jc, pY=pY, pZ=pZ, t=t):
            te.wait_ge(ts.h, tok[(t, "xn_fm")])
            if t == 0 and jc == 0:
                te.wait_ge(ld.h, e["pre_ld"])
            if jc >= 2:
                te.wait_ge(ss.h, tok[(t, "z_evac", jc - 2)])
                te.wait_ge(sv.h, tok[(t, "zy", jc - 2)])
                te.wait_ge(ss.h, tok[(t, "y_evac", jc - 2)])
            elif t > 0:
                te.wait_ge(sv.h, tok[(t - 1, "dh_done")])
            for dc in range(8):
                mm = te.matmul(pY[:], e["tv_sb"][:, dc * D + jc * 128: dc * D + jc * 128 + 128],
                               xn_fm[:, dc * RS:(dc + 1) * RS],
                               start=(dc == 0), stop=(dc == 7))
            pe.inc(mm)
            for dc in range(8):
                mm = te.matmul(pZ[:], e["tk_sb"][:, dc * D + jc * 128: dc * D + jc * 128 + 128],
                               xn_fm[:, dc * RS:(dc + 1) * RS],
                               start=(dc == 0), stop=(dc == 7))
            pe.inc(mm)
        tok[(t, "yz", jc)] = pe.v

        @block.scalar
        def _(s, jc=jc, pY=pY, t=t):
            s.wait_ge(pe.h, tok[(t, "yz", jc)] - 1)
            if jc >= 1:
                s.wait_ge(sv.h, tok[(t, "zy", jc - 1)])
            elif t > 0:
                s.wait_ge(sv.h, tok[(t - 1, "zy", 7)])
            ss.inc(s.activation(y_sb[:, 0:RS], pY[:], AF.Copy))
        tok[(t, "y_evac", jc)] = ss.v

        @block.vector
        def _(v, jc=jc, pZ=pZ, t=t):
            v.wait_ge(pe.h, tok[(t, "yz", jc)])
            v.wait_ge(ss.h, tok[(t, "y_evac", jc)])
            if t > 0 and jc == 0:
                v.wait_ge(ts.h, tok[(t - 1, "zb_T")])
                v.wait_ge(sv.h, tok[(t - 1, "dr_done", 7)])
            sv.inc(v.tensor_tensor(zy_bf[:, jc * RS:(jc + 1) * RS], pZ[:],
                                   y_sb[:, 0:RS], A.subtract))
        tok[(t, "zy", jc)] = sv.v

        @block.scalar
        def _(s, jc=jc, pZ=pZ, t=t):
            s.wait_ge(sv.h, tok[(t, "zy", jc)])
            if t > 0 and jc == 0:
                s.wait_ge(pe.h, tok[(t - 1, "hpre", 1)])
            ss.inc(s.activation(z_bf[:, jc * RS:(jc + 1) * RS], pZ[:], AF.Copy))
        tok[(t, "z_evac", jc)] = ss.v

    @block.gpsimd
    def _(g, t=t):
        g.wait_ge(ss.h, tok[(t, "z_evac", 7)])
        g.wait_ge(sp.h, sp.v)
        g.dma_start(out=e["zb_h"].rearrange("(dc p) r -> p dc r", p=128),
                    in_=z_bf.rearrange("p (dc r) -> p dc r", dc=8)).then_inc(sp.h, 16)
    sp.v += 16
    tok[(t, "zb_sp")] = sp.v

    @block.scalar
    def _(s, t=t):
        s.wait_ge(sp.h, tok[(t, "zb_sp")])
        if t > 0:
            s.wait_ge(pe.h, tok[(t - 1, "g_done")])
        s.wait_ge(ts.h, ts.v)
        s.dma_start_transpose(e["z_tm"].rearrange("p (rt d) -> p rt d", rt=4),
                              e["zb_h"][:, :]).then_inc(ts.h, 16)
    ts.v += 16
    tok[(t, "zb_T")] = ts.v

    for jc in range(8):
        pX = pC if jc % 2 == 0 else pD

        @block.tensor
        def _(te, jc=jc, pX=pX, t=t):
            if jc >= 2:
                te.wait_ge(ss.h, tok[(t, "qfm_evac", jc - 2)])
            else:
                te.wait_ge(sv.h, tok[(t, "zy", 7)])
                te.wait_ge(ss.h, tok[(t, "z_evac", 7)])
            for dc in range(8):
                mm = te.matmul(pX[:], e["tq_sb"][:, dc * D + jc * 128: dc * D + jc * 128 + 128],
                               xn_fm[:, dc * RS:(dc + 1) * RS],
                               start=(dc == 0), stop=(dc == 7))
            pe.inc(mm)
        tok[(t, "qfm", jc)] = pe.v

        @block.scalar
        def _(s, jc=jc, pX=pX, t=t):
            s.wait_ge(pe.h, tok[(t, "qfm", jc)])
            if t >= 2 and jc == 0:
                s.wait_ge(pe.h, tok[(t - 2, "h2_done")])
            ss.inc(s.activation(q_bf[:, jc * RS:(jc + 1) * RS], pX[:], AF.Copy))
        tok[(t, "qfm_evac", jc)] = ss.v


def emit_step(e, t):
    block, tok = e["block"], e["tok"]
    (ld, ts, sp, sv, ss, pe, cc) = (e[k] for k in ["ld", "ts", "sp", "sv", "ss", "pe", "cc"])
    xn_fm = e["xn_fm"][t % 2]
    q_bf = e["q_bf"][t % 2]
    (z_bf, zy_bf) = (e["z_bf"], e["zy_bf"])
    (w1t_m, w2_m, w2t_m, w1t_b, w2_b, w2t_b) = (
        e[k] for k in ["w1t_m", "w2_m", "w2t_m", "w1t_b", "w2_b", "w2t_b"])
    (b1_m, b2_m, b2row, ones1) = (e["b1_m"], e["b2_m"], e["b2row"], e["ones1"])
    (hpre, h_f, h_bf, dr_bf, dh_bf, h2_bf) = (
        e[k] for k in ["hpre", "h_f", "h_bf", "dr_bf", "dh_bf", "h2_bf"])
    (z_tm, dr_tm, h_tm, dh_tm) = (e["z_tm"], e["dr_tm"], e["h_tm"], e["dh_tm"])
    (scrA, scrB, scrC, gsend, s_sb) = (
        e[k] for k in ["scrA", "scrB", "scrC", "gsend", "s_sb"])
    (pA, pB, pC, pD, pG) = (e["pA"], e["pB"], e["pC"], e["pD"], e["pG"])
    r0 = RS * t

    for cc2 in range(2):
        pX = pA if cc2 == 0 else pB

        @block.tensor
        def _(te, cc2=cc2, pX=pX, t=t):
            te.wait_ge(ss.h, tok[(t, "z_evac", 7)])
            te.wait_ge(sv.h, tok[(t, "zy", 7)])
            if cc2 == 0:
                te.wait_ge(sv.h, e["pre_sv"] if t == 0 else tok[(t - 1, "upd_done")])
                if t > 0:
                    te.wait_ge(ss.h, tok[(t - 1, "h2_evac", 1)])
            for jc in range(8):
                mm = te.matmul(pX[:], w1t_b[:, jc * C + cc2 * 128: jc * C + cc2 * 128 + 128],
                               z_bf[:, jc * RS:(jc + 1) * RS],
                               start=(jc == 0), stop=(jc == 7))
            pe.inc(mm)
        tok[(t, "hpre", cc2)] = pe.v

        @block.scalar
        def _(s, cc2=cc2, pX=pX, t=t):
            s.wait_ge(pe.h, tok[(t, "hpre", cc2)])
            if t > 0 and cc2 == 0:
                s.wait_ge(sv.h, tok[(t - 1, "dh_done")])
            ss.inc(s.activation(hpre[:, cc2 * RS:(cc2 + 1) * RS], pX[:], AF.Identity,
                                bias=b1_m[:, cc2:cc2 + 1]))
            ss.inc(s.activation(h_f[:, cc2 * RS:(cc2 + 1) * RS], pX[:], AF.Gelu,
                                bias=b1_m[:, cc2:cc2 + 1]))
        tok[(t, "h_evac", cc2)] = ss.v

        @block.vector
        def _(v, cc2=cc2, t=t):
            v.wait_ge(ss.h, tok[(t, "h_evac", cc2)])
            if t > 0 and cc2 == 0:
                v.wait_ge(ts.h, tok[(t - 1, "hb_T")])
            sv.inc(v.tensor_copy(h_bf[:, cc2 * RS:(cc2 + 1) * RS],
                                 h_f[:, cc2 * RS:(cc2 + 1) * RS]))
        tok[(t, "h_bf", cc2)] = sv.v

    @block.gpsimd
    def _(g, t=t):
        g.wait_ge(sv.h, tok[(t, "h_bf", 1)])
        g.wait_ge(sp.h, sp.v)
        g.dma_start(out=e["hb_h"].rearrange("(cc p) r -> p cc r", p=128),
                    in_=h_bf.rearrange("p (cc r) -> p cc r", cc=2)).then_inc(sp.h, 16)
    sp.v += 16
    tok[(t, "hb_sp")] = sp.v

    @block.scalar
    def _(s, t=t):
        s.wait_ge(sp.h, tok[(t, "hb_sp")])
        if t > 0:
            s.wait_ge(pe.h, tok[(t - 1, "g_done")])
        s.wait_ge(ts.h, ts.v)
        s.dma_start_transpose(e["h_tm"].rearrange("p (rt c) -> p rt c", rt=4),
                              e["hb_h"][:, :]).then_inc(ts.h, 16)
    ts.v += 16
    tok[(t, "hb_T")] = ts.v

    for dc in range(8):
        pX = pC if dc % 2 == 0 else pD

        @block.tensor
        def _(te, dc=dc, pX=pX, t=t):
            te.wait_ge(sv.h, tok[(t, "h_bf", 1)])
            if dc < 2:
                te.wait_ge(ss.h, tok[(t, "qfm_evac", 7)])
            else:
                te.wait_ge(sv.h, tok[(t, "dr_done", dc - 2)])
            for cc2 in range(2):
                mm = te.matmul(pX[:], w2t_b[:, cc2 * D + dc * 128: cc2 * D + dc * 128 + 128],
                               h_bf[:, cc2 * RS:(cc2 + 1) * RS],
                               start=(cc2 == 0), stop=(cc2 == 1))
            pe.inc(mm)
        tok[(t, "rmm", dc)] = pe.v

        @block.vector
        def _(v, dc=dc, pX=pX, t=t):
            v.wait_ge(pe.h, tok[(t, "rmm", dc)])
            if t > 0 and dc == 0:
                v.wait_ge(ts.h, tok[(t - 1, "drb_T")])
            sv.inc(v.scalar_tensor_tensor(dr_bf[:, dc * RS:(dc + 1) * RS], pX[:],
                                          b2_m[:, dc:dc + 1],
                                          zy_bf[:, dc * RS:(dc + 1) * RS], A.add, A.add))
            v.wait_ge(sv.h, sv.v)
            with e["nc"].allow_low_precision(reason="bf16 grad-transfer payload"):
                sv.inc(v.reduce_sum(gsend[:, 6146 + dc:6147 + dc],
                                    dr_bf[:, dc * RS:(dc + 1) * RS], axis=AX))
        tok[(t, "dr_done", dc)] = sv.v

    @block.gpsimd
    def _(g, t=t):
        g.wait_ge(sv.h, tok[(t, "dr_done", 7)])
        g.wait_ge(sp.h, sp.v)
        g.dma_start(out=e["drb_h"].rearrange("(dc p) r -> p dc r", p=128),
                    in_=dr_bf.rearrange("p (dc r) -> p dc r", dc=8)).then_inc(sp.h, 16)
    sp.v += 16
    tok[(t, "drb_sp")] = sp.v

    @block.scalar
    def _(s, t=t):
        s.wait_ge(sp.h, tok[(t, "drb_sp")])
        if t > 0:
            s.wait_ge(pe.h, tok[(t - 1, "g_done")])
        s.wait_ge(ts.h, ts.v)
        s.dma_start_transpose(e["dr_tm"].rearrange("p (rt d) -> p rt d", rt=4),
                              e["drb_h"][:, :]).then_inc(ts.h, 16)
    ts.v += 16
    tok[(t, "drb_T")] = ts.v

    for cc2 in range(2):
        pX = pA if cc2 == 0 else pB

        @block.tensor
        def _(te, cc2=cc2, pX=pX, t=t):
            te.wait_ge(sv.h, tok[(t, "dr_done", 7)])
            te.wait_ge(ss.h, tok[(t, "h_evac", cc2)])
            for dc in range(8):
                mm = te.matmul(pX[:], w2_b[:, dc * C + cc2 * 128: dc * C + cc2 * 128 + 128],
                               dr_bf[:, dc * RS:(dc + 1) * RS],
                               start=(dc == 0), stop=(dc == 7))
            pe.inc(mm)
        tok[(t, "dhmm", cc2)] = pe.v

    @block.scalar
    def _(s, t=t):
        s.wait_ge(sv.h, tok[(t, "h_bf", 1)])
        if t > 0:
            s.wait_ge(sp.h, tok[(t - 1, "s_spill", 1)])
        ss.inc(s.activation(scrA[:], hpre[:], AF.Square))
        s.wait_ge(ss.h, ss.v)
        ss.inc(s.activation(scrB[:], scrA[:], AF.Exp, scale=-0.5))
    tok[(t, "ephi")] = ss.v

    @block.vector
    def _(v, t=t):
        def tv(i):
            v.wait_ge(sv.h, sv.inc(i))
        v.wait_ge(ss.h, tok[(t, "ephi")])
        v.wait_ge(pe.h, tok[(t, "dhmm", 1)])
        tv(v.tensor_scalar(scrC[:], hpre[:], 0.0, None, A.is_equal))
        tv(v.tensor_tensor(scrA[:], hpre[:], scrC[:], A.add))
        tv(v.reciprocal(scrA[:], scrA[:]))
        tv(v.tensor_tensor(scrA[:], h_f[:], scrA[:], A.mult))
        tv(v.scalar_tensor_tensor(scrA[:], scrC[:], 0.5, scrA[:], A.mult, A.add))
        tv(v.tensor_tensor(scrB[:], scrB[:], hpre[:], A.mult))
        tv(v.scalar_tensor_tensor(scrB[:], scrB[:], 0.3989422804014327, scrA[:],
                                  A.mult, A.add))
        if t > 0:
            v.wait_ge(ts.h, tok[(t - 1, "dhb_T")])
        for cc2 in range(2):
            pX = pA if cc2 == 0 else pB
            tv(v.tensor_tensor(dh_bf[:, cc2 * RS:(cc2 + 1) * RS], pX[:],
                               scrB[:, cc2 * RS:(cc2 + 1) * RS], A.mult))
            with e["nc"].allow_low_precision(reason="bf16 grad-transfer payload"):
                tv(v.reduce_sum(gsend[:, 6144 + cc2:6145 + cc2],
                                dh_bf[:, cc2 * RS:(cc2 + 1) * RS], axis=AX))
    tok[(t, "dh_done")] = sv.v

    @block.gpsimd
    def _(g, t=t):
        g.wait_ge(sv.h, tok[(t, "dh_done")])
        g.wait_ge(sp.h, sp.v)
        g.dma_start(out=e["dhb_h"].rearrange("(cc p) r -> p cc r", p=128),
                    in_=dh_bf.rearrange("p (cc r) -> p cc r", cc=2)).then_inc(sp.h, 16)
    sp.v += 16
    tok[(t, "dhb_sp")] = sp.v

    @block.scalar
    def _(s, t=t):
        s.wait_ge(sp.h, tok[(t, "dhb_sp")])
        if t > 0:
            s.wait_ge(pe.h, tok[(t - 1, "g_done")])
        s.wait_ge(ts.h, ts.v)
        s.dma_start_transpose(e["dh_tm"].rearrange("p (rt c) -> p rt c", rt=4),
                              e["dhb_h"][:, :]).then_inc(ts.h, 16)
    ts.v += 16
    tok[(t, "dhb_T")] = ts.v

    @block.tensor
    def _(te, t=t):
        te.wait_ge(ts.h, tok[(t, "zb_T")])
        te.wait_ge(ts.h, tok[(t, "dhb_T")])
        if t > 0:
            te.wait_ge(sv.h, tok[(t - 1, "s_evac", 1)])
        for dc in range(8):
            for rc in range(4):
                mm = te.matmul(pG[:, dc * C:(dc + 1) * C],
                               z_tm[:, rc * D + dc * 128: rc * D + dc * 128 + 128],
                               dh_tm[:, rc * C:(rc + 1) * C],
                               start=(rc == 0), stop=(rc == 3))
        pe.inc(mm)
    tok[(t, "gw1t")] = pe.v

    @block.vector
    def _(v, t=t):
        v.wait_ge(pe.h, tok[(t, "gw1t")])
        sv.inc(v.tensor_copy(gsend[:, 0:GW], pG[:]))
    tok[(t, "gw1t_evac")] = sv.v

    @block.tensor
    def _(te, t=t):
        te.wait_ge(ts.h, tok[(t, "drb_T")])
        te.wait_ge(ts.h, tok[(t, "hb_T")])
        te.wait_ge(sv.h, tok[(t, "gw1t_evac")])
        for dc in range(8):
            for rc in range(4):
                mm = te.matmul(pG[:, dc * C:(dc + 1) * C],
                               dr_tm[:, rc * D + dc * 128: rc * D + dc * 128 + 128],
                               h_tm[:, rc * C:(rc + 1) * C],
                               start=(rc == 0), stop=(rc == 3))
        pe.inc(mm)
    tok[(t, "gw2")] = pe.v

    @block.vector
    def _(v, t=t):
        v.wait_ge(pe.h, tok[(t, "gw2")])
        sv.inc(v.tensor_copy(gsend[:, GW:2 * GW], pG[:]))
    tok[(t, "gw2_evac")] = sv.v

    @block.tensor
    def _(te, t=t):
        te.wait_ge(sv.h, tok[(t, "gw2_evac")])
        for cc2 in range(2):
            for dh2 in range(2):
                for rc in range(4):
                    mm = te.matmul(pG[:, cc2 * D + dh2 * 512: cc2 * D + dh2 * 512 + 512],
                                   h_tm[:, rc * C + cc2 * 128: rc * C + cc2 * 128 + 128],
                                   dr_tm[:, rc * D + dh2 * 512: rc * D + dh2 * 512 + 512],
                                   start=(rc == 0), stop=(rc == 3))
        pe.inc(mm)
    tok[(t, "gw2t")] = pe.v
    tok[(t, "g_done")] = pe.v

    @block.vector
    def _(v, t=t):
        v.wait_ge(pe.h, tok[(t, "gw2t")])
        sv.inc(v.tensor_copy(gsend[:, 2 * GW:3 * GW], pG[:]))
    tok[(t, "gw2t_evac")] = sv.v

    @block.gpsimd
    def _(g, t=t):
        g.wait_ge(sv.h, tok[(t, "gw2t_evac")])
        g.wait_ge(sv.h, tok[(t, "dh_done")])
        g.wait_ge(sp.h, sp.v)
        g.dma_start(out=e["ar_in"][:, :], in_=gsend[:]).then_inc(sp.h, 16)
        g.wait_ge(sp.h, sp.v + 16)
        cc.inc(g.collective_compute(
            "AllReduce", A.add, replica_groups=[list(range(NCORES))],
            ins=[e["ar_in"].flatten()[None, :].opt()],
            outs=[e["ar_out"].flatten()[None, :].opt()]))
        g.wait_ge(cc.h, cc.v)
        g.dma_start(out=gsend[:], in_=e["ar_out"][:, :]).then_inc(sp.h, 16)
    sp.v += 32
    tok[(t, "ar_done")] = sp.v

    if t + 1 < B:
        emit_front(e, t + 1)

    @block.vector
    def _(v, t=t):
        def tv(i):
            v.wait_ge(sv.h, sv.inc(i))
        v.wait_ge(sp.h, tok[(t, "ar_done")])
        tv(v.scalar_tensor_tensor(w1t_m[:], gsend[:, 0:GW], -LR2, w1t_m[:], A.mult, A.add))
        tv(v.scalar_tensor_tensor(b1_m[:], gsend[:, 6144:6146], -LR2, b1_m[:], A.mult, A.add))
        tv(v.tensor_copy(w1t_b[:], w1t_m[:]))
    tok[(t, "upd_w1t")] = sv.v

    @block.vector
    def _(v, t=t):
        def tv(i):
            v.wait_ge(sv.h, sv.inc(i))
        tv(v.scalar_tensor_tensor(w2_m[:], gsend[:, GW:2 * GW], -LR2, w2_m[:], A.mult, A.add))
        tv(v.tensor_copy(w2_b[:], w2_m[:]))
        tv(v.scalar_tensor_tensor(w2t_m[:], gsend[:, 2 * GW:3 * GW], -LR2, w2t_m[:],
                                  A.mult, A.add))
        tv(v.tensor_copy(w2t_b[:], w2t_m[:]))
        tv(v.scalar_tensor_tensor(b2_m[:], gsend[:, 6146:6154], -LR2, b2_m[:], A.mult, A.add))
    tok[(t, "upd_done")] = sv.v

    @block.sync
    def _(sy, t=t):
        sy.wait_ge(sv.h, tok[(t, "upd_done")])
        if t > 0:
            sy.wait_ge(pe.h, tok[(t - 1, "xn_consumed")])
        sy.wait_ge(ld.h, ld.v)
        sy.dma_start(out=e["b2r_h"].rearrange("(dc p) -> p dc", p=128), in_=b2_m[:],
                     allow_slow_non_contiguous=True).then_inc(ld.h, 16)
        sy.wait_ge(ld.h, ld.v + 16)
        sy.dma_start(out=b2row[0:1, :], in_=e["b2r_h"][None, :]).then_inc(ld.h, 16)
    ld.v += 32
    tok[(t, "b2row_done_ld")] = ld.v

    for cc2 in range(2):
        pX = pA if cc2 == 0 else pB

        @block.tensor
        def _(te, cc2=cc2, pX=pX, t=t):
            te.wait_ge(sv.h, tok[(t, "upd_w1t")])
            te.wait_ge(ss.h, tok[(t, "qfm_evac", 7)])
            te.wait_ge(sv.h, tok[(t, "dh_done")])
            if t + 1 < B and cc2 == 0:
                te.wait_ge(ss.h, tok[(t + 1, "z_evac", 7)])
                te.wait_ge(sv.h, tok[(t + 1, "zy", 7)])
                te.wait_ge(ss.h, tok[(t + 1, "qfm_evac", 7)])
            for jc in range(8):
                mm = te.matmul(pX[:], w1t_b[:, jc * C + cc2 * 128: jc * C + cc2 * 128 + 128],
                               q_bf[:, jc * RS:(jc + 1) * RS],
                               start=(jc == 0), stop=(jc == 7))
            pe.inc(mm)
        tok[(t, "h2mm", cc2)] = pe.v

        @block.scalar
        def _(s, cc2=cc2, pX=pX, t=t):
            s.wait_ge(pe.h, tok[(t, "h2mm", cc2)])
            if t > 0 and cc2 == 0:
                s.wait_ge(pe.h, tok[(t - 1, "xn_consumed")])
            ss.inc(s.activation(h2_bf[:, cc2 * RS:(cc2 + 1) * RS], pX[:], AF.Gelu,
                                bias=b1_m[:, cc2:cc2 + 1]))
        tok[(t, "h2_evac", cc2)] = ss.v
    tok[(t, "h2_done")] = pe.v

    for half in range(2):
        for rc in (2 * half, 2 * half + 1):
            slot = rc % 2

            @block.tensor
            def _(te, rc=rc, slot=slot, half=half, t=t):
                te.wait_ge(ss.h, tok[(t, "h2_evac", 1)])
                te.wait_ge(sv.h, tok[(t, "upd_done")])
                te.wait_ge(sv.h, tok[(t, "gw2t_evac")])
                te.wait_ge(ld.h, tok[(t, "b2row_done_ld")])
                if half == 1 and rc == 2:
                    te.wait_ge(sv.h, tok[(t, "s_evac", 0)])
                for jh in range(2):
                    reg = pG[:, slot * D + jh * 512: slot * D + jh * 512 + 512]
                    for dc in range(8):
                        te.matmul(reg,
                                  xn_fm[:, dc * RS + rc * 128: dc * RS + rc * 128 + 128],
                                  e["tq_sb"][:, dc * D + jh * 512: dc * D + jh * 512 + 512],
                                  start=(dc == 0), stop=False)
                    for cc2 in range(2):
                        te.matmul(reg,
                                  h2_bf[:, cc2 * RS + rc * 128: cc2 * RS + rc * 128 + 128],
                                  w2t_b[:, cc2 * D + jh * 512: cc2 * D + jh * 512 + 512],
                                  start=False, stop=False)
                    mm = te.matmul(reg, ones1[:], b2row[0:1, jh * 512: jh * 512 + 512],
                                   start=False, stop=True)
                pe.inc(mm)
            tok[(t, "s_mm", rc)] = pe.v
        if half == 1:
            tok[(t, "xn_consumed")] = pe.v

        @block.vector
        def _(v, half=half, t=t):
            for rc in (2 * half, 2 * half + 1):
                slot = rc % 2
                v.wait_ge(pe.h, tok[(t, "s_mm", rc)])
                if half == 0 and t > 0:
                    v.wait_ge(sp.h, tok[(t - 1, "s_spill", 1)])
                if half == 1:
                    v.wait_ge(sp.h, tok[(t, "s_spill", 0)])
                sv.inc(v.tensor_copy(s_sb[:, slot * D:(slot + 1) * D],
                                     pG[:, slot * D:(slot + 1) * D]))
        tok[(t, "s_evac", half)] = sv.v

        @block.gpsimd
        def _(g, half=half, t=t):
            g.wait_ge(sv.h, tok[(t, "s_evac", half)])
            for i, rc in enumerate((2 * half, 2 * half + 1)):
                slot = rc % 2
                g.wait_ge(sp.h, sp.v + 16 * i)
                g.dma_start(out=e["s_hbm"][r0 + rc * 128: r0 + (rc + 1) * 128, :],
                            in_=s_sb[:, slot * D:(slot + 1) * D]).then_inc(sp.h, 16)
        sp.v += 32
        tok[(t, "s_spill", half)] = sp.v



NB5 = (NI + 511) // 512


def build_s4(fused=None):
    P = fused or Prog("s4")
    nc = P.nc
    xq_hbm = P.inp("xq_bf", (B * RS, D), BF16)
    wg_h = P.inp("wqg_full", (NI, D), BF16)
    wu_h = P.inp("wqu_full", (NI, D), BF16)
    hq_hbm = P.out("hq_bf", (B * RS, NI), BF16)

    wgT = P.sb("s4_wgT", [128, 8 * NI], BF16)
    wuT = P.sb("s4_wuT", [128, 8 * NI], BF16)
    xq_fm = [P.sb(f"s4_xqfm{i}", [128, 8 * RS], BF16) for i in range(2)]
    gs = P.sb("s4_gs", [128, NI], F32)
    hf = [P.sb(f"s4_hf{i}", [128, NI], F32) for i in range(3)]
    hqb = [P.sb(f"s4_hqb{i}", [128, NI], BF16) for i in range(2)]
    HW4 = NI // 2
    scrq = [quant_scratch3(P, HW4, "s4a"), quant_scratch3(P, HW4, "s4b")]
    pp = [P.ps(f"s4_p{i}", [128, 512]) for i in range(8)]

    ts = P.sem("s4_ts")
    sv = P.sem("s4_sv")
    gv = P.sem("s4_gv")
    ss = P.sem("s4_ss")
    pe = P.sem("s4_pe")
    sp = P.sem("s4_sp")

    tok = {}
    pc = [0]
    pfree = {}
    block = P.get_block()

    @block.scalar
    def _(s):
        kk = [0]
        for dst, srch in [(wgT, wg_h), (wuT, wu_h)]:
            for k in range(4):
                n0 = k * (NI // 4)
                s.wait_ge(ts.h, ts.v + 16 * kk[0]); kk[0] += 1
                s.dma_start_transpose(
                    dst.rearrange("p (dc n) -> p dc n", dc=8)[:, :, n0:n0 + NI // 4],
                    srch[n0:n0 + NI // 4, :]).then_inc(ts.h, 16)
    ts.v += 8 * 16
    w_tok = ts.v

    for ch in range(8):
        cpar = ch % 2

        @block.scalar
        def _(s, ch=ch, cpar=cpar):
            if ch >= 2:
                s.wait_ge(pe.h, tok[("xq_used", ch - 2)])
            s.wait_ge(ts.h, ts.v)
            s.dma_start_transpose(xq_fm[cpar].rearrange("p (dc r) -> p dc r", dc=8),
                                  xq_hbm[512 * ch:512 * (ch + 1), :]).then_inc(ts.h, 16)
        ts.v += 16
        tok[("xq_fm", ch)] = ts.v

        for tt in range(4):
            t_idx = 4 * ch + tt
            tpar = t_idx % 3
            hpar = t_idx % 2

            phases = [(0, (0, 1)), (1, (2, 3)), (2, (4, 5))]
            gpair = {}
            upair = {}
            for ph, nbs in phases:
                pr = pc[0] % 4
                pc[0] += 1
                gpair[ph] = pr

                @block.tensor
                def _(te, ch=ch, tt=tt, cpar=cpar, t_idx=t_idx, ph=ph, nbs=nbs, pr=pr):
                    te.wait_ge(ts.h, tok[("xq_fm", ch)])
                    if pr in pfree:
                        sem, val = pfree[pr]
                        te.wait_ge(sem.h, val)
                    for dc in range(8):
                        for k, nb in enumerate(nbs):
                            w = min(512, NI - nb * 512)
                            mm = te.matmul(pp[2 * pr + k][:, 0:w],
                                           xq_fm[cpar][:, dc * RS + tt * 128: dc * RS + tt * 128 + 128],
                                           wgT[:, dc * NI + nb * 512: dc * NI + nb * 512 + w],
                                           start=(dc == 0), stop=(dc == 7))
                    pe.inc(mm)
                tok[("gate", t_idx, ph)] = pe.v

                @block.scalar
                def _(s, t_idx=t_idx, ph=ph, nbs=nbs, pr=pr):
                    s.wait_ge(pe.h, tok[("gate", t_idx, ph)])
                    if ("hmul", t_idx - 1, ph) in tok:
                        s.wait_ge(sv.h, tok[("hmul", t_idx - 1, ph)])
                    for k, nb in enumerate(nbs):
                        w = min(512, NI - nb * 512)
                        ss.inc(s.activation(gs[:, nb * 512: nb * 512 + w],
                                            pp[2 * pr + k][:, 0:w], AF.Silu))
                tok[("gs", t_idx, ph)] = ss.v
                pfree[pr] = (ss, ss.v)

            for ph, nbs in phases:
                pr = pc[0] % 4
                pc[0] += 1
                upair[ph] = pr

                @block.tensor
                def _(te, ch=ch, tt=tt, cpar=cpar, t_idx=t_idx, ph=ph, nbs=nbs, pr=pr):
                    if pr in pfree:
                        sem, val = pfree[pr]
                        te.wait_ge(sem.h, val)
                    for dc in range(8):
                        for k, nb in enumerate(nbs):
                            w = min(512, NI - nb * 512)
                            mm = te.matmul(pp[2 * pr + k][:, 0:w],
                                           xq_fm[cpar][:, dc * RS + tt * 128: dc * RS + tt * 128 + 128],
                                           wuT[:, dc * NI + nb * 512: dc * NI + nb * 512 + w],
                                           start=(dc == 0), stop=(dc == 7))
                    pe.inc(mm)
                tok[("up", t_idx, ph)] = pe.v

                @block.vector
                def _(v, t_idx=t_idx, tpar=tpar, ph=ph, nbs=nbs, pr=pr):
                    v.wait_ge(pe.h, tok[("up", t_idx, ph)])
                    v.wait_ge(ss.h, tok[("gs", t_idx, ph)])
                    if ph == 0 and ("hq_spill", t_idx - 3) in tok:
                        v.wait_ge(sp.h, tok[("hq_spill", t_idx - 3)])
                    for k, nb in enumerate(nbs):
                        w = min(512, NI - nb * 512)
                        sv.inc(v.tensor_tensor(hf[tpar][:, nb * 512: nb * 512 + w],
                                               pp[2 * pr + k][:, 0:w],
                                               gs[:, nb * 512: nb * 512 + w], A.mult))
                    v.wait_ge(sv.h, sv.v)
                tok[("hmul", t_idx, ph)] = sv.v
                pfree[pr] = (sv, sv.v)
            tok[("h_done", t_idx)] = sv.v

            tok[("xq_used", ch)] = pe.v
            qsems = {"sv": sv, "ss": ss, "gv": gv}

            def ogd(g, t_idx=t_idx):
                if ("hq_spill", t_idx - 2) in tok:
                    g.wait_ge(sp.h, tok[("hq_spill", t_idx - 2)])

            def emit_spill(tile, par):
                @block.gpsimd
                def _(g, tile=tile, par=par):
                    g.wait_ge(gv.h, gv.v)
                    g.wait_ge(sp.h, sp.v)
                    g.dma_start(out=hq_hbm[128 * tile:128 * (tile + 1), :],
                                in_=hqb[par][:]).then_inc(sp.h, 16)
                sp.v += 16
                tok[("hq_spill", tile)] = sp.v

            q3_pre(block, qsems, scrq[0], hf[tpar][:, 0:HW4], HW4)
            if "q_pend" in tok:
                pt, ppar = tok.pop("q_pend")
                q3_ladder(block, qsems, scrq[1], hqb[ppar][:, HW4:2 * HW4], HW4,
                          out_guard=ogd)
                emit_spill(pt, ppar)
            q3_pre(block, qsems, scrq[1], hf[tpar][:, HW4:2 * HW4], HW4)
            q3_ladder(block, qsems, scrq[0], hqb[hpar][:, 0:HW4], HW4, out_guard=ogd)
            tok["q_pend"] = (t_idx, hpar)

    if "q_pend" in tok:
        pt, ppar = tok.pop("q_pend")

        def ogd2(g, pt=pt):
            if ("hq_spill", pt - 2) in tok:
                g.wait_ge(sp.h, tok[("hq_spill", pt - 2)])
        q3_ladder(block, {"sv": sv, "ss": ss, "gv": gv}, scrq[1],
                  hqb[ppar][:, HW4:2 * HW4], HW4, out_guard=ogd2)

        @block.gpsimd
        def _(g, pt=pt, ppar=ppar):
            g.wait_ge(gv.h, gv.v)
            g.wait_ge(sp.h, sp.v)
            g.dma_start(out=hq_hbm[128 * pt:128 * (pt + 1), :],
                        in_=hqb[ppar][:]).then_inc(sp.h, 16)
        sp.v += 16

    @block.gpsimd
    def _(g):
        g.wait_ge(sp.h, sp.v)

    return P



def build_s5(fused=None):
    P = fused or Prog("s5")
    nc = P.nc
    hq_hbm = P.inp("hq_bf", (B * RS, NI), BF16)
    wd_h = P.inp("wqd_full", (D, NI), BF16)
    s_hbm = P.inp("s_f32", (B * RS, D), F32)
    x_hbm = P.inp("x_shard2", (B * RS, D), F32)
    out_hbm = P.out("out_shard", (B * RS, D), F32)

    wdT = P.sb("s5_wdT", [128, 22 * D], BF16)
    hq_fm = [P.sb(f"s5_hqfm{i}", [128, 22 * RS], BF16) for i in range(2)]
    x_sb = [P.sb(f"s5_x{i}", [128, D], F32) for i in range(2)]
    s_sb = [P.sb(f"s5_s{i}", [128, D], F32) for i in range(2)]
    g_sb = [P.sb(f"s5_g{i}", [128, D], F32) for i in range(2)]
    o_sb = [P.sb(f"s5_o{i}", [128, D], F32) for i in range(2)]
    pp = [P.ps(f"s5_p{i}", [128, 512]) for i in range(8)]

    ld = P.sem("s5_ld")
    ts = P.sem("s5_ts")
    sv = P.sem("s5_sv")
    ss = P.sem("s5_ss")
    pe = P.sem("s5_pe")
    sp = P.sem("s5_sp")

    tok = {}
    block = P.get_block()

    @block.scalar
    def _(s):
        for k in range(2):
            d0 = k * 512
            s.wait_ge(ts.h, ts.v + 16 * k)
            s.dma_start_transpose(
                wdT.rearrange("p (nc d) -> p nc d", nc=22)[:, :, d0:d0 + 512],
                wd_h[d0:d0 + 512, :]).then_inc(ts.h, 16)
    ts.v += 32
    w_tok = ts.v

    for ch in range(8):
        cpar = ch % 2

        @block.scalar
        def _(s, ch=ch, cpar=cpar):
            if ch >= 2:
                s.wait_ge(pe.h, tok[("hq_used", ch - 2)])
            s.wait_ge(ts.h, ts.v)
            s.dma_start_transpose(hq_fm[cpar].rearrange("p (nc r) -> p nc r", nc=22),
                                  hq_hbm[512 * ch:512 * (ch + 1), :]).then_inc(ts.h, 16)
        ts.v += 16
        tok[("hq_fm", ch)] = ts.v

        for tt in range(4):
            tpar = tt % 2
            qpar = (4 * ch + tt) % 4
            t_idx = 4 * ch + tt

            @block.sync
            def _(sy, t_idx=t_idx, tpar=tpar):
                if ("out_done", t_idx - 2) in tok:
                    sy.wait_ge(sv.h, tok[("out_done", t_idx - 2)])
                sy.wait_ge(ld.h, ld.v)
                sy.dma_start(out=x_sb[tpar][:],
                             in_=x_hbm[128 * t_idx:128 * (t_idx + 1), :]).then_inc(ld.h, 16)
                sy.wait_ge(ld.h, ld.v + 16)
                sy.dma_start(out=s_sb[tpar][:],
                             in_=s_hbm[128 * t_idx:128 * (t_idx + 1), :]).then_inc(ld.h, 16)
            ld.v += 32
            tok[("xs_ld", t_idx)] = ld.v

            @block.tensor
            def _(te, ch=ch, tt=tt, cpar=cpar, t_idx=t_idx, qpar=qpar):
                te.wait_ge(ts.h, tok[("hq_fm", ch)])
                if ("g_evac", t_idx - 3) in tok:
                    te.wait_ge(ss.h, tok[("g_evac", t_idx - 3)])
                for nc2 in range(22):
                    for dh2 in range(2):
                        mm = te.matmul(
                            pp[2 * qpar + dh2][:],
                            hq_fm[cpar][:, nc2 * RS + tt * 128: nc2 * RS + tt * 128 + 128],
                            wdT[:, nc2 * D + dh2 * 512: nc2 * D + dh2 * 512 + 512],
                            start=(nc2 == 0), stop=(nc2 == 21))
                pe.inc(mm)
            tok[("down", t_idx)] = pe.v
            tok[("hq_used", ch)] = pe.v

            @block.scalar
            def _(s, t_idx=t_idx, tpar=tpar, qpar=qpar):
                s.wait_ge(pe.h, tok[("down", t_idx)])
                if ("out_done", t_idx - 2) in tok:
                    s.wait_ge(sv.h, tok[("out_done", t_idx - 2)])
                for dh2 in range(2):
                    ss.inc(s.activation(g_sb[tpar][:, dh2 * 512:(dh2 + 1) * 512],
                                        pp[2 * qpar + dh2][:], AF.Silu))
            tok[("g_evac", t_idx)] = ss.v

            @block.vector
            def _(v, t_idx=t_idx, tpar=tpar):
                v.wait_ge(ss.h, tok[("g_evac", t_idx)])
                v.wait_ge(ld.h, tok[("xs_ld", t_idx)])
                if ("out_spill", t_idx - 2) in tok:
                    v.wait_ge(sp.h, tok[("out_spill", t_idx - 2)])
                sv.inc(v.tensor_tensor(o_sb[tpar][:], s_sb[tpar][:], g_sb[tpar][:], A.mult))
                v.wait_ge(sv.h, sv.v)
                sv.inc(v.tensor_tensor(o_sb[tpar][:], o_sb[tpar][:], x_sb[tpar][:], A.add))
            tok[("out_done", t_idx)] = sv.v

            @block.gpsimd
            def _(g, t_idx=t_idx, tpar=tpar):
                g.wait_ge(sv.h, tok[("out_done", t_idx)])
                g.wait_ge(sp.h, sp.v)
                g.dma_start(out=out_hbm[128 * t_idx:128 * (t_idx + 1), :],
                            in_=o_sb[tpar][:]).then_inc(sp.h, 16)
            sp.v += 16
            tok[("out_spill", t_idx)] = sp.v

    @block.gpsimd
    def _(g):
        g.wait_ge(sp.h, sp.v)

    return P


# version 22
# speedup vs baseline: 1.0318x; 1.0318x over previous
import contextlib
import numpy as np

import concourse.bass as bass
import concourse.mybir as mybir
from concourse.bass_utils import run_bass_kernel_spmd

F32 = mybir.dt.float32
BF16 = mybir.dt.bfloat16
I32 = mybir.dt.int32
A = mybir.AluOpType
AF = mybir.ActivationFunctionType
AX = mybir.AxisListType.X

B, S, D, C, NI = 8, 4096, 1024, 256, 2816
NCORES = 8
RS = S // NCORES
NT = B * RS // 128
LR2 = 0.001 * 2.0 / float(S * D)
INV6 = 1.0 / 6.0
SIGNBIT = -2147483648


class Sem:
    def __init__(self, es, nc, name):
        self.h = es.enter_context(nc.semaphore(name))
        self.name = name
        self.v = 0

    def inc(self, inst, amt=1):
        inst.then_inc(self.h, amt)
        self.v += amt
        return self.v


REAL_INPUTS = {"x_shard", "wg_shard", "wu_shard", "wd_shard", "tkT", "tvT", "tqT",
               "W1T", "W2f", "W2T", "b1c", "b2c"}


class Prog:
    def __init__(self, name, fused=False):
        self.nc = bass.Bass("TRN2", target_bir_lowering=False)
        self.es = contextlib.ExitStack()
        self.cur_es = self.es
        self.name = name
        self.fused = fused
        self.outs = []
        self.tensors = {}
        self.all_sems = []
        self.block = None

    def get_block(self):
        if self.block is None:
            self.block = self.es.enter_context(self.nc.Block())
        return self.block

    def inp(self, name, shape, dt):
        if self.fused and name == "x_shard2":
            return self.tensors["x_shard"]
        if name in self.tensors:
            return self.tensors[name]
        kind = "ExternalInput" if (not self.fused or name in REAL_INPUTS) else "Internal"
        if kind == "ExternalInput":
            ap = self.nc.dram_tensor(name, list(shape), dt, kind="ExternalInput").ap()
        else:
            ap = self.nc.dram_tensor(name, list(shape), dt).ap()
        self.tensors[name] = ap
        return ap

    def out(self, name, shape, dt):
        if self.fused and name != "out_shard":
            if name in self.tensors:
                return self.tensors[name]
            ap = self.nc.dram_tensor(name, list(shape), dt).ap()
            self.tensors[name] = ap
            return ap
        self.outs.append(name)
        ap = self.nc.dram_tensor(name, list(shape), dt, kind="ExternalOutput").ap()
        self.tensors[name] = ap
        return ap

    def dram(self, name, shape, dt, addr_space="Local"):
        return self.nc.dram_tensor(name, list(shape), dt, addr_space=addr_space).ap()

    def sb(self, name, shape, dt):
        return self.cur_es.enter_context(self.nc.sbuf_tensor(name, list(shape), dt))

    def ps(self, name, shape):
        return self.cur_es.enter_context(self.nc.psum_tensor(name, list(shape), F32))

    def sem(self, name):
        s = Sem(self.es, self.nc, name)
        self.all_sems.append(s)
        return s


def emit_barrier(P, skip=()):
    block = P.get_block()
    sems = [s for s in P.all_sems if s.v > 0 and s.name not in skip]

    for engname in ["tensor", "vector", "scalar", "gpsimd", "sync"]:
        @block.__getattribute__(engname)
        def _(e):
            for s in sems:
                e.wait_ge(s.h, s.v)


def emit_quant_ladder(v, sv, ss_val, ss, scr, pdim, W, out_bf_ap, bitwise=True):
    nb = W // 16
    p = pdim
    an, qa, qb, sn = scr["an"], scr["qa"], scr["qb"], scr["sn"]

    def tv(i):
        v.wait_ge(sv.h, sv.inc(i))

    v.wait_ge(ss.h, ss_val)
    tv(v.tensor_scalar(qa[0:p, 0:W], an[0:p, 0:W], 0.25, None, A.is_gt))
    tv(v.scalar_tensor_tensor(qa[0:p, 0:W], an[0:p, 0:W], 0.75, qa[0:p, 0:W], A.is_gt, A.add))
    tv(v.scalar_tensor_tensor(qa[0:p, 0:W], an[0:p, 0:W], 1.25, qa[0:p, 0:W], A.is_gt, A.add))
    tv(v.scalar_tensor_tensor(qa[0:p, 0:W], an[0:p, 0:W], 1.75, qa[0:p, 0:W], A.is_gt, A.add))
    tv(v.tensor_scalar(qb[0:p, 0:W], an[0:p, 0:W], 5.0, 2.0, A.is_gt, A.mult))
    tv(v.scalar_tensor_tensor(qb[0:p, 0:W], an[0:p, 0:W], 2.5, qb[0:p, 0:W], A.is_gt, A.add))
    tv(v.scalar_tensor_tensor(qb[0:p, 0:W], an[0:p, 0:W], 3.5, qb[0:p, 0:W], A.is_gt, A.add))
    tv(v.scalar_tensor_tensor(qb[0:p, 0:W], qa[0:p, 0:W], 0.5, qb[0:p, 0:W], A.mult, A.add))
    sbb = scr["sbs"][0:p, 0:nb, None].broadcast_to([p, nb, 16])
    tv(v.tensor_tensor(an[0:p, 0:W].rearrange("p (nb b) -> p nb b", b=16),
                       qb[0:p, 0:W].rearrange("p (nb b) -> p nb b", b=16), sbb, A.mult))
    if bitwise:
        tv(v.tensor_scalar(qa[0:p, 0:W].bitcast(I32), sn[0:p, 0:W].bitcast(I32),
                           SIGNBIT, None, A.bitwise_and))
        tv(v.tensor_tensor(an[0:p, 0:W].bitcast(I32), an[0:p, 0:W].bitcast(I32),
                           qa[0:p, 0:W].bitcast(I32), A.bitwise_or))
    else:
        tv(v.tensor_scalar(qa[0:p, 0:W], sn[0:p, 0:W], 0.0, 2.0, A.is_ge, A.mult))
        tv(v.tensor_scalar(qa[0:p, 0:W], qa[0:p, 0:W], 1.0, None, A.subtract))
        tv(v.tensor_tensor(an[0:p, 0:W], an[0:p, 0:W], qa[0:p, 0:W], A.mult))
    tv(v.tensor_copy(out_bf_ap, an[0:p, 0:W]))


def emit_quant_pre(v, sv, src_f32, scr, pdim, W):
    emit_quant_scales(v, sv, src_f32, scr, pdim, W)
    emit_quant_sn(v, sv, src_f32, scr, pdim, W)


def emit_quant_scales(v, sv, src_f32, scr, pdim, W):
    nb = W // 16
    p = pdim
    x3 = src_f32[0:p, 0:W].rearrange("p (nb b) -> p nb b", b=16)

    def tv(i):
        v.wait_ge(sv.h, sv.inc(i))

    tv(v.tensor_reduce(scr["m"][0:p, 0:nb], x3, axis=AX, op=A.max,
                       apply_absolute_value=True))
    tv(v.tensor_scalar(scr["sbs"][0:p, 0:nb], scr["m"][0:p, 0:nb], INV6, 1e-12,
                       A.mult, A.max))
    tv(v.reciprocal(scr["rs"][0:p, 0:nb], scr["sbs"][0:p, 0:nb]))


def emit_quant_sn(v, sv, src_f32, scr, pdim, W):
    nb = W // 16
    p = pdim
    x3 = src_f32[0:p, 0:W].rearrange("p (nb b) -> p nb b", b=16)
    rsb = scr["rs"][0:p, 0:nb, None].broadcast_to([p, nb, 16])
    v.tensor_tensor(scr["sn"][0:p, 0:W].rearrange("p (nb b) -> p nb b", b=16),
                    x3, rsb, A.mult).then_inc(sv.h, 1)
    sv.v += 1
    v.wait_ge(sv.h, sv.v)


def quant_scratch3(P, W, tag=""):
    scr = {}
    for k in ["sn", "an"]:
        scr[k] = P.sb(f"q{tag}_{k}", [128, W], F32)
    for k in ["rA", "rB", "qv", "sgn"]:
        scr[k] = P.sb(f"q{tag}_{k}", [128, W], BF16)
    for k in ["m", "sbs", "rs", "sb2"]:
        scr[k] = P.sb(f"q{tag}_{k}", [128, W // 16], F32)
    return scr


def q3_pre(block, sems, scr, src_f32, W, pdim=128, extra_v_wait=None):
    sv, ss, gv = sems["sv"], sems["ss"], sems["gv"]
    nb = W // 16
    p = pdim
    sn, an, sgn = scr["sn"], scr["an"], scr["sgn"]
    war_gv = scr.get("war_gv", 0)

    @block.vector
    def _(v):
        def tv(i):
            v.wait_ge(sv.h, sv.inc(i))
        if extra_v_wait is not None:
            extra_v_wait(v)
        v.wait_ge(gv.h, war_gv)
        x3 = src_f32[0:p, 0:W].rearrange("p (n b) -> p n b", b=16)
        tv(v.tensor_reduce(scr["m"][0:p, 0:nb], x3, axis=AX, op=A.max,
                           apply_absolute_value=True))
        tv(v.tensor_scalar(scr["sbs"][0:p, 0:nb], scr["m"][0:p, 0:nb], INV6, 1e-12,
                           A.mult, A.max))
        tv(v.reciprocal(scr["rs"][0:p, 0:nb], scr["sbs"][0:p, 0:nb]))
        tv(v.tensor_scalar(scr["sb2"][0:p, 0:nb], scr["sbs"][0:p, 0:nb], 0.5, None,
                           A.mult))
    scales_tok = sv.v

    @block.gpsimd
    def _(g):
        g.wait_ge(sv.h, scales_tok)
        g.wait_ge(ss.h, scr.get("war_ss", 0))
        x3 = src_f32[0:p, 0:W].rearrange("p (n b) -> p n b", b=16)
        rsb = scr["rs"][0:p, 0:nb, None].broadcast_to([p, nb, 16])
        gv.inc(g.tensor_tensor(sn[0:p, 0:W].rearrange("p (n b) -> p n b", b=16),
                               x3, rsb, A.mult))
    sn_tok = gv.v
    scr["sn_done_gv"] = gv.v

    @block.scalar
    def _(s):
        s.wait_ge(gv.h, sn_tok)
        s.wait_ge(sv.h, scr.get("war_sv", 0))
        ss.inc(s.activation(an[0:p, 0:W], sn[0:p, 0:W], AF.Abs))
        ss.inc(s.activation(sgn[0:p, 0:W], sn[0:p, 0:W], AF.Sign))
    scr["abs_ss"] = ss.v
    return ss.v


def q3_ladder(block, sems, scr, out_bf, W, pdim=128, out_guard=None):
    sv, ss, gv = sems["sv"], sems["ss"], sems["gv"]
    nb = W // 16
    p = pdim
    an, rA, rB, qv, sgn = scr["an"], scr["rA"], scr["rB"], scr["qv"], scr["sgn"]
    abs_tok = scr["abs_ss"]

    @block.vector
    def _(v):
        def tv(i):
            v.wait_ge(sv.h, sv.inc(i))
        v.wait_ge(ss.h, abs_tok)
        v.wait_ge(gv.h, scr.get("war2_gv", 0))
        tv(v.tensor_scalar(rA[0:p, 0:W], an[0:p, 0:W], 0.25, None, A.is_gt))
        tv(v.scalar_tensor_tensor(rA[0:p, 0:W], an[0:p, 0:W], 0.75, rA[0:p, 0:W],
                                  A.is_gt, A.add))
        tv(v.scalar_tensor_tensor(rA[0:p, 0:W], an[0:p, 0:W], 1.25, rA[0:p, 0:W],
                                  A.is_gt, A.add))
        tv(v.scalar_tensor_tensor(rA[0:p, 0:W], an[0:p, 0:W], 1.75, rA[0:p, 0:W],
                                  A.is_gt, A.add))
        tv(v.tensor_scalar(rB[0:p, 0:W], an[0:p, 0:W], 5.0, 2.0, A.is_gt, A.mult))
        tv(v.scalar_tensor_tensor(rB[0:p, 0:W], an[0:p, 0:W], 2.5, rB[0:p, 0:W],
                                  A.is_gt, A.add))
        tv(v.scalar_tensor_tensor(rB[0:p, 0:W], an[0:p, 0:W], 3.5, rB[0:p, 0:W],
                                  A.is_gt, A.add))
        tv(v.scalar_tensor_tensor(rA[0:p, 0:W], rB[0:p, 0:W], 2.0, rA[0:p, 0:W],
                                  A.mult, A.add))
    scr["war_sv"] = sv.v
    ladder_tok = sv.v

    @block.gpsimd
    def _(g):
        g.wait_ge(sv.h, ladder_tok)
        sbb = scr["sb2"][0:p, 0:nb, None].broadcast_to([p, nb, 16])
        gv.inc(g.tensor_tensor(qv[0:p, 0:W].rearrange("p (n b) -> p n b", b=16),
                               rA[0:p, 0:W].rearrange("p (n b) -> p n b", b=16),
                               sbb, A.mult))
        g.wait_ge(gv.h, gv.v)
        if out_guard is not None:
            out_guard(g)
        gv.inc(g.tensor_tensor(out_bf, qv[0:p, 0:W], sgn[0:p, 0:W], A.mult))
    scr["war_gv"] = gv.v - 1
    scr["war2_gv"] = gv.v
    scr["war_ss"] = scr["abs_ss"]
    return gv.v


def emit_quant3(block, sems, scr, src_f32, out_bf, W, extra_v_wait=None,
                extra_g_wait=None, out_guard=None, pdim=128):
    q3_pre(block, sems, scr, src_f32, W, pdim=pdim, extra_v_wait=extra_v_wait)
    return q3_ladder(block, sems, scr, out_bf, W, pdim=pdim, out_guard=out_guard)


def quant_scratch(P, W, tag=""):
    scr = {k: P.sb(f"q{tag}_{k}", [128, W], F32) for k in ["sn", "an", "qa", "qb"]}
    scr["m"] = P.sb(f"q{tag}_m", [128, W // 16], F32)
    scr["sbs"] = P.sb(f"q{tag}_sbs", [128, W // 16], F32)
    scr["rs"] = P.sb(f"q{tag}_rs", [128, W // 16], F32)
    return scr



def build_s1(fused=None):
    P = fused or Prog("s1")
    nc = P.nc
    x = P.inp("x_shard", (B * RS, D), F32)
    xn_hbm = P.out("xn_bf", (B * RS, D), BF16)
    xq_hbm = P.out("xq_bf", (B * RS, D), BF16)

    xa = [P.sb(f"s1_xa{i}", [128, D], F32) for i in range(2)]
    dif = [P.sb(f"s1_dif{i}", [128, D], F32) for i in range(2)]
    xnt = [P.sb(f"s1_xnt{i}", [128, D], F32) for i in range(2)]
    xnb = [P.sb(f"s1_xnb{i}", [128, D], BF16) for i in range(2)]
    xqb = [P.sb(f"s1_xqb{i}", [128, D], BF16) for i in range(2)]
    scrx = [quant_scratch3(P, D, f"s1q{i}") for i in range(2)]
    cols = P.sb("s1_cols", [128, 8], F32)

    ld = P.sem("s1_ld")
    sv = P.sem("s1_sv")
    ss = P.sem("s1_ss")
    gd = P.sem("s1_gd")
    gv = P.sem("s1_gv")
    qsems = {"sv": sv, "ss": ss, "gv": gv}

    xa_free = {}
    xn_spill = {}
    xq_spill = {}
    tok = {}
    block = P.get_block()

    def emit_ladder_and_spill(t):
        par = t % 2

        def out_guard(g, t=t):
            if t >= 2:
                g.wait_ge(gd.h, xq_spill[t - 2])
        q3_ladder(block, qsems, scrx[t % 2], xqb[par][:], D, out_guard=out_guard)

        @block.gpsimd
        def _(g, t=t, par=par):
            g.wait_ge(gv.h, gv.v)
            g.wait_ge(gd.h, gd.v)
            g.dma_start(out=xq_hbm[128 * t:128 * (t + 1), :], in_=xqb[par][:])\
             .then_inc(gd.h, 16)
        gd.v += 16
        xq_spill[t] = gd.v

    cols2 = [cols, P.sb("s1_cols2", [128, 8], F32)]
    sqjunk = P.sb("s1_sqjunk", [128, D], F32)

    def emit_load(t):
        par = t % 2

        @block.sync
        def _(sy, t=t, par=par):
            if t >= 2:
                sy.wait_ge(sv.h, xa_free[t - 2])
            sy.wait_ge(ld.h, ld.v)
            sy.dma_start(out=xa[par][:], in_=x[128 * t:128 * (t + 1), :])\
              .then_inc(ld.h, 16)
        ld.v += 16
        ld_tok[t] = ld.v

    def emit_ln_a(t):
        par = t % 2
        cl = cols2[t % 2]

        @block.vector
        def _(v, t=t, par=par, cl=cl):
            def tv(i):
                v.wait_ge(sv.h, sv.inc(i))
            v.wait_ge(ld.h, ld_tok[t])
            if t >= 2:
                v.wait_ge(ss.h, sq_tok.get(t - 2, 0))
            tv(v.reduce_sum(cl[:, 0:1], xa[par][:], axis=AX))
            tv(v.tensor_scalar(cl[:, 1:2], cl[:, 0:1], 1.0 / D, None, A.mult))
            tv(v.tensor_scalar(dif[par][:], xa[par][:], cl[:, 1:2], None, A.subtract))
            xa_free[t] = sv.v
        dif_tok[t] = sv.v

        @block.scalar
        def _(s, t=t, par=par, cl=cl):
            s.wait_ge(sv.h, dif_tok[t])
            ss.inc(s.activation(sqjunk[:], dif[par][:], AF.Square,
                                accum_out=cl[:, 2:3]))
        sq_tok[t] = ss.v

    def emit_ln_b(t):
        par = t % 2
        cl = cols2[t % 2]

        @block.vector
        def _(v, t=t, par=par, cl=cl):
            def tv(i):
                v.wait_ge(sv.h, sv.inc(i))
            v.wait_ge(ss.h, sq_tok[t])
            tv(v.tensor_scalar(cl[:, 3:4], cl[:, 2:3], 1.0 / D, 1e-5, A.mult, A.add))
            tv(v.reciprocal(cl[:, 4:5], cl[:, 3:4]))
        rv_tok[t] = sv.v

        @block.scalar
        def _(s, t=t, cl=cl):
            s.wait_ge(sv.h, rv_tok[t])
            ss.inc(s.activation(cl[:, 5:6], cl[:, 4:5], AF.Sqrt))
        sqrt_tok[t] = ss.v

        @block.vector
        def _(v, t=t, par=par, cl=cl):
            def tv(i):
                v.wait_ge(sv.h, sv.inc(i))
            v.wait_ge(ss.h, sqrt_tok[t])
            if t >= 2:
                v.wait_ge(gv.h, scrx[t % 2].get("sn_done_gv", 0))
            tv(v.tensor_scalar(xnt[par][:], dif[par][:], cl[:, 5:6], None, A.mult))
            if t >= 2:
                v.wait_ge(gd.h, xn_spill[t - 2])
            tv(v.tensor_copy(xnb[par][:], xnt[par][:]))

        @block.gpsimd
        def _(g, t=t, par=par):
            g.wait_ge(sv.h, sv.v)
            g.wait_ge(gd.h, gd.v)
            g.dma_start(out=xn_hbm[128 * t:128 * (t + 1), :], in_=xnb[par][:])\
             .then_inc(gd.h, 16)
        gd.v += 16
        xn_spill[t] = gd.v

    ld_tok = {}
    dif_tok = {}
    sq_tok = {}
    rv_tok = {}
    sqrt_tok = {}
    emit_load(0)
    emit_ln_a(0)
    for t in range(NT):
        if t + 1 < NT:
            emit_load(t + 1)
            emit_ln_a(t + 1)
        emit_ln_b(t)
        q3_pre(block, qsems, scrx[t % 2], xnt[t % 2][:], D)
        if t >= 1:
            emit_ladder_and_spill(t - 1)

    emit_ladder_and_spill(NT - 1)

    @block.gpsimd
    def _(g):
        g.wait_ge(gd.h, gd.v)

    return P


def host_inputs_s1(inputs, core):
    x = np.ascontiguousarray(
        inputs["L0_attn_in_seq"][:, RS * core:RS * (core + 1), :]).reshape(B * RS, D)
    return {"x_shard": x}


def build_fused():
    P = Prog("mega", fused=True)
    P.get_block()
    for gi, group in enumerate(((build_s1, build_s2), (build_s3,), (build_s4,),
                                (build_s5,))):
        st = contextlib.ExitStack()
        P.cur_es = st
        for builder in group:
            builder(P)
        emit_barrier(P, skip=("s2_cc",) if gi == 0 else ())
        st.close()
    P.cur_es = P.es
    return P


_PROG = None


def _get_prog():
    global _PROG
    if _PROG is None:
        _PROG = build_fused()
    return _PROG


def host_core_inputs(inputs, c):
    m = host_inputs_s1(inputs, c)
    m.update(host_inputs_s2(inputs, c))
    m.update({
        "W1T": np.ascontiguousarray(inputs["L0_attn_W1"].T),
        "W2f": np.ascontiguousarray(inputs["L0_attn_W2"]),
        "W2T": np.ascontiguousarray(inputs["L0_attn_W2"].T),
        "b1c": np.ascontiguousarray(np.asarray(inputs["L0_attn_b1"]).reshape(2, 128).T),
        "b2c": np.ascontiguousarray(np.asarray(inputs["L0_attn_b2"]).reshape(8, 128).T),
    })
    return m


def kernel(**inputs):
    P = _get_prog()
    cores = list(range(NCORES))
    r = run_bass_kernel_spmd(P.nc, [host_core_inputs(inputs, c) for c in cores],
                             core_ids=cores)
    out = np.empty((B, S, D), np.float32)
    for c in cores:
        out[:, RS * c:RS * (c + 1), :] = r.results[c]["out_shard"].reshape(B, RS, D)
    return out



WG_SH = NI // NCORES
WD_SH = D // NCORES


def build_s2(fused=None):
    P = fused or Prog("s2")
    nc = P.nc
    wg_sh = P.inp("wg_shard", (WG_SH, D), F32)
    wu_sh = P.inp("wu_shard", (WG_SH, D), F32)
    wd_sh = P.inp("wd_shard", (WD_SH, NI), F32)
    tkT = P.inp("tkT", (D, D), F32)
    tvT = P.inp("tvT", (D, D), F32)
    tqT = P.inp("tqT", (D, D), F32)

    wqg_full = P.out("wqg_full", (NI, D), BF16)
    wqu_full = P.out("wqu_full", (NI, D), BF16)
    wqd_full = P.out("wqd_full", (D, NI), BF16)
    tkTb = P.out("tkTb", (D, D), BF16)
    tvTb = P.out("tvTb", (D, D), BF16)
    tqTb = P.out("tqTb", (D, D), BF16)

    g_self = P.dram("s2_g_self", (WG_SH, D), BF16)
    u_self = P.dram("s2_u_self", (WG_SH, D), BF16)
    d_self = P.dram("s2_d_self", (WD_SH, NI), BF16)
    if P.fused:
        g_all, u_all, d_all = wqg_full, wqu_full, wqd_full
    else:
        g_all = P.dram("s2_g_all", (NI, D), BF16, addr_space="Shared")
        u_all = P.dram("s2_u_all", (NI, D), BF16, addr_space="Shared")
        d_all = P.dram("s2_d_all", (D, NI), BF16, addr_space="Shared")

    wa = [P.sb(f"s2_wa{i}", [128, NI], F32) for i in range(2)]
    wb = [P.sb(f"s2_wb{i}", [128, NI], BF16) for i in range(2)]
    scr2 = [quant_scratch3(P, NI, f"s2q{i}") for i in range(2)]

    ld = P.sem("s2_ld")
    sv = P.sem("s2_sv")
    ss = P.sem("s2_ss")
    gd = P.sem("s2_gd")
    gv = P.sem("s2_gv")
    cc = P.sem("s2_cc")
    qsems = {"sv": sv, "ss": ss, "gv": gv}

    block = P.get_block()

    tiles = []
    for srct, dst, rows, W in [(wg_sh, g_self, WG_SH, D), (wu_sh, u_self, WG_SH, D),
                               (wd_sh, d_self, WD_SH, NI)]:
        r0 = 0
        while r0 < rows:
            p = min(128, rows - r0)
            tiles.append((srct, dst, r0, p, W))
            r0 += p

    ld_tok = {}
    spill_tok = {}

    def emit_load(k):
        srct, dst, r0, p, W = tiles[k]

        @block.sync
        def _(sy, k=k, srct=srct, r0=r0, p=p, W=W):
            if k >= 2:
                sy.wait_ge(gv.h, scr2[k % 2].get("sn_done_gv", 0))
                sy.wait_ge(sv.h, sv.v)
            sy.wait_ge(ld.h, ld.v)
            sy.dma_start(out=wa[k % 2][0:p, 0:W], in_=srct[r0:r0 + p, :])\
              .then_inc(ld.h, 16)
        ld.v += 16
        ld_tok[k] = ld.v

    def emit_quant_k(k):
        srct, dst, r0, p, W = tiles[k]

        def ewv(v, k=k):
            v.wait_ge(ld.h, ld_tok[k])
        q3_pre(block, qsems, scr2[k % 2], wa[k % 2][0:p, 0:W], W, pdim=p,
               extra_v_wait=ewv)

    def emit_ladder_k(k):
        srct, dst, r0, p, W = tiles[k]

        def og(g, k=k):
            if k >= 2:
                g.wait_ge(gd.h, spill_tok[k - 2])
        q3_ladder(block, qsems, scr2[k % 2], wb[k % 2][0:p, 0:W], W, pdim=p,
                  out_guard=og)

        @block.gpsimd
        def _(g, k=k, dst=dst, r0=r0, p=p, W=W):
            g.wait_ge(gv.h, gv.v)
            g.wait_ge(gd.h, gd.v)
            g.dma_start(out=dst[r0:r0 + p, :], in_=wb[k % 2][0:p, 0:W]).then_inc(gd.h, 16)
        gd.v += 16
        spill_tok[k] = gd.v

    emit_load(0)
    for k in range(len(tiles)):
        if k + 1 < len(tiles):
            emit_load(k + 1)
        emit_quant_k(k)
        if k >= 1:
            emit_ladder_k(k - 1)
    emit_ladder_k(len(tiles) - 1)

    for src, dst in [(tkT, tkTb), (tvT, tvTb), (tqT, tqTb)]:
        for tchunk in range(D // 128):
            @block.sync
            def _(sy, src=src, tchunk=tchunk):
                sy.wait_ge(gd.h, gd.v)
                sy.wait_ge(ld.h, ld.v)
                sy.dma_start(out=wa[0][:, 0:D], in_=src[128 * tchunk:128 * (tchunk + 1), :])\
                  .then_inc(ld.h, 16)
            ld.v += 16
            ld_tok = ld.v

            @block.vector
            def _(v, ld_tok=ld_tok):
                v.wait_ge(ld.h, ld_tok)
                v.wait_ge(gd.h, gd.v)
                sv.inc(v.tensor_copy(wb[0][:, 0:D], wa[0][:, 0:D]))
                v.wait_ge(sv.h, sv.v)

            @block.gpsimd
            def _(g, dst=dst, tchunk=tchunk):
                g.wait_ge(sv.h, sv.v)
                g.wait_ge(gd.h, gd.v)
                g.dma_start(out=dst[128 * tchunk:128 * (tchunk + 1), :], in_=wb[0][:, 0:D])\
                 .then_inc(gd.h, 16)
            gd.v += 16

    @block.gpsimd
    def _(g):
        g.wait_ge(gd.h, gd.v)
        for self_t, all_t in [(g_self, g_all), (u_self, u_all), (d_self, d_all)]:
            cc.inc(g.collective_compute(
                "AllGather", A.bypass,
                replica_groups=[list(range(NCORES))],
                ins=[self_t.flatten()[None, :].opt()],
                outs=[all_t.flatten()[None, :].opt()],
            ))
        if not P.fused:
            g.wait_ge(cc.h, cc.v)
            base = gd.v
            for i, (all_t, ext) in enumerate([(g_all, wqg_full), (u_all, wqu_full),
                                              (d_all, wqd_full)]):
                g.wait_ge(gd.h, base + 16 * i)
                g.dma_start(out=ext[:, :], in_=all_t[:, :]).then_inc(gd.h, 16)
            gd.v += 48

    @block.gpsimd
    def _(g):
        g.wait_ge(gd.h, gd.v)

    return P


def host_inputs_s2(inputs, core):
    wg = np.ascontiguousarray(inputs["L0_ffn_w_gate"][WG_SH * core:WG_SH * (core + 1)])
    wu = np.ascontiguousarray(inputs["L0_ffn_w_up"][WG_SH * core:WG_SH * (core + 1)])
    wd = np.ascontiguousarray(inputs["L0_ffn_w_down"][WD_SH * core:WD_SH * (core + 1)])
    return {
        "wg_shard": wg, "wu_shard": wu, "wd_shard": wd,
        "tkT": np.ascontiguousarray(inputs["L0_attn_t_k"].T),
        "tvT": np.ascontiguousarray(inputs["L0_attn_t_v"].T),
        "tqT": np.ascontiguousarray(inputs["L0_attn_t_q"].T),
    }



GW = 2048
ARW = 3 * GW + 2 + 8


def build_s3(fused=None):
    P = fused or Prog("s3")
    nc = P.nc
    e = {}
    e["xn_hbm"] = P.inp("xn_bf", (B * RS, D), BF16)
    tk_h = P.inp("tkTb", (D, D), BF16)
    tv_h = P.inp("tvTb", (D, D), BF16)
    tq_h = P.inp("tqTb", (D, D), BF16)
    w1t_h = P.inp("W1T", (D, C), F32)
    w2_h = P.inp("W2f", (D, C), F32)
    w2t_h = P.inp("W2T", (C, D), F32)
    b1_h = P.inp("b1c", (128, 2), F32)
    b2_h = P.inp("b2c", (128, 8), F32)
    e["s_hbm"] = P.out("s_f32", (B * RS, D), F32)

    e["zb_h"] = P.dram("s3_zb", (D, RS), BF16)
    e["hb_h"] = P.dram("s3_hb", (C, RS), BF16)
    e["drb_h"] = P.dram("s3_drb", (D, RS), BF16)
    e["dhb_h"] = P.dram("s3_dhb", (C, RS), BF16)
    e["b2r_h"] = P.dram("s3_b2r", (D,), F32)
    e["ar_in"] = P.dram("s3_ar_in", (128, ARW), BF16)
    e["ar_out"] = P.dram("s3_ar_out", (128, ARW), BF16, addr_space="Shared")

    e["tk_sb"] = P.sb("s3_tk", [128, 8 * D], BF16)
    e["tv_sb"] = P.sb("s3_tv", [128, 8 * D], BF16)
    e["tq_sb"] = P.sb("s3_tq", [128, 8 * D], BF16)
    e["w1t_m"] = P.sb("s3_w1t_m", [128, GW], F32)
    e["w2_m"] = P.sb("s3_w2_m", [128, GW], F32)
    e["w2t_m"] = P.sb("s3_w2t_m", [128, GW], F32)
    e["w1t_b"] = P.sb("s3_w1t_b", [128, GW], BF16)
    e["w2_b"] = P.sb("s3_w2_b", [128, GW], BF16)
    e["w2t_b"] = P.sb("s3_w2t_b", [128, GW], BF16)
    e["b1_m"] = P.sb("s3_b1_m", [128, 2], F32)
    e["b2_m"] = P.sb("s3_b2_m", [128, 8], F32)
    e["b2row"] = P.sb("s3_b2row", [1, D], F32)
    e["ones1"] = P.sb("s3_ones1", [1, 128], F32)
    e["xn_fm"] = [P.sb(f"s3_xn_fm{i}", [128, 8 * RS], BF16) for i in range(2)]
    e["q_bf"] = [P.sb(f"s3_q_bf{i}", [128, 8 * RS], BF16) for i in range(2)]
    e["z_bf"] = P.sb("s3_z_bf", [128, 8 * RS], BF16)
    e["zy_bf"] = P.sb("s3_zy_bf", [128, 8 * RS], BF16)
    e["y_sb"] = P.sb("s3_y_sb", [128, RS], F32)
    e["hpre"] = P.sb("s3_hpre", [128, 2 * RS], F32)
    e["h_f"] = P.sb("s3_h_f", [128, 2 * RS], F32)
    e["h_bf"] = P.sb("s3_h_bf", [128, 2 * RS], BF16)
    e["dr_bf"] = P.sb("s3_dr_bf", [128, 8 * RS], BF16)
    e["dh_bf"] = P.sb("s3_dh_bf", [128, 2 * RS], BF16)
    e["h2_bf"] = P.sb("s3_h2_bf", [128, 2 * RS], BF16)
    e["z_tm"] = P.sb("s3_z_tm", [128, 4 * D], BF16)
    e["dr_tm"] = P.sb("s3_dr_tm", [128, 4 * D], BF16)
    e["h_tm"] = P.sb("s3_h_tm", [128, 4 * C], BF16)
    e["dh_tm"] = P.sb("s3_dh_tm", [128, 4 * C], BF16)
    pool = P.sb("s3_pool", [128, 3 * D], F32)
    e["scrA"] = pool[:, 0:D]
    e["scrB"] = pool[:, D:2 * D]
    e["scrC"] = pool[:, 2 * D:3 * D]
    e["s_sb"] = pool[:, 0:2 * D]
    e["gsend"] = P.sb("s3_gsend", [128, ARW], BF16)

    e["pA"] = P.ps("s3_pA", [128, 512])
    e["pB"] = P.ps("s3_pB", [128, 512])
    e["pC"] = P.ps("s3_pC", [128, 512])
    e["pD"] = P.ps("s3_pD", [128, 512])
    e["pG"] = P.ps("s3_pG", [128, GW])

    for s in ["ld", "ts", "sp", "sv", "ss", "pe", "cc"]:
        e[s] = P.sem("s3_" + s)
    (ld, ts, sp, sv, ss, pe, cc) = (e[k] for k in ["ld", "ts", "sp", "sv", "ss", "pe", "cc"])

    block = P.get_block()
    e["block"] = block
    e["nc"] = nc
    e["tok"] = {}

    @block.sync
    def _(sy):
        k = [0]
        def gate():
            sy.wait_ge(ld.h, ld.v + 16 * k[0])
            k[0] += 1
        for dst, srch in [(e["tk_sb"], tk_h), (e["tv_sb"], tv_h), (e["tq_sb"], tq_h)]:
            gate()
            sy.dma_start(out=dst.rearrange("p (dc j) -> p dc j", dc=8),
                         in_=srch.rearrange("(dc p) j -> p dc j", p=128))\
              .then_inc(ld.h, 16)
        gate()
        sy.dma_start(out=e["w1t_m"].rearrange("p (dc c) -> p dc c", dc=8),
                     in_=w1t_h.rearrange("(dc p) c -> p dc c", p=128)).then_inc(ld.h, 16)
        gate()
        sy.dma_start(out=e["w2_m"].rearrange("p (dc c) -> p dc c", dc=8),
                     in_=w2_h.rearrange("(dc p) c -> p dc c", p=128)).then_inc(ld.h, 16)
        gate()
        sy.dma_start(out=e["w2t_m"].rearrange("p (cc d) -> p cc d", cc=2),
                     in_=w2t_h.rearrange("(cc p) d -> p cc d", p=128)).then_inc(ld.h, 16)
        gate()
        sy.dma_start(out=e["b1_m"][:], in_=b1_h[:, :]).then_inc(ld.h, 16)
        gate()
        sy.dma_start(out=e["b2_m"][:], in_=b2_h[:, :]).then_inc(ld.h, 16)
    ld.v += 8 * 16
    e["pre_ld"] = ld.v

    @block.vector
    def _(v):
        def tv(i):
            v.wait_ge(sv.h, sv.inc(i))
        v.wait_ge(ld.h, e["pre_ld"])
        tv(v.memset(e["ones1"][:], 1.0))
        tv(v.tensor_copy(e["w1t_b"][:], e["w1t_m"][:]))
        tv(v.tensor_copy(e["w2_b"][:], e["w2_m"][:]))
        tv(v.tensor_copy(e["w2t_b"][:], e["w2t_m"][:]))
    e["pre_sv"] = sv.v

    @block.sync
    def _(sy):
        sy.wait_ge(ld.h, e["pre_ld"])
        sy.dma_start(out=e["b2r_h"].rearrange("(dc p) -> p dc", p=128), in_=e["b2_m"][:],
                     allow_slow_non_contiguous=True).then_inc(ld.h, 16)
        sy.wait_ge(ld.h, ld.v + 16)
        sy.dma_start(out=e["b2row"][0:1, :], in_=e["b2r_h"][None, :]).then_inc(ld.h, 16)
    ld.v += 32

    emit_front(e, 0)
    for t in range(B):
        emit_step(e, t)

    @block.gpsimd
    def _(g):
        g.wait_ge(sp.h, sp.v)

    return P


def emit_front(e, t):
    block, tok = e["block"], e["tok"]
    (ld, ts, sp, sv, ss, pe, cc) = (e[k] for k in ["ld", "ts", "sp", "sv", "ss", "pe", "cc"])
    xn_fm = e["xn_fm"][t % 2]
    q_bf = e["q_bf"][t % 2]
    (z_bf, zy_bf, y_sb) = (e["z_bf"], e["zy_bf"], e["y_sb"])
    (pA, pB, pC, pD) = (e["pA"], e["pB"], e["pC"], e["pD"])
    r0 = RS * t

    @block.scalar
    def _(s, t=t):
        if t >= 2:
            s.wait_ge(pe.h, tok[(t - 2, "xn_consumed")])
        s.wait_ge(ts.h, ts.v)
        s.dma_start_transpose(xn_fm.rearrange("p (dc r) -> p dc r", dc=8),
                              e["xn_hbm"][r0:r0 + RS, :]).then_inc(ts.h, 16)
    ts.v += 16
    tok[(t, "xn_fm")] = ts.v

    for jc in range(8):
        pY = pA if jc % 2 == 0 else pC
        pZ = pB if jc % 2 == 0 else pD

        @block.tensor
        def _(te, jc=jc, pY=pY, pZ=pZ, t=t):
            te.wait_ge(ts.h, tok[(t, "xn_fm")])
            if t == 0 and jc == 0:
                te.wait_ge(ld.h, e["pre_ld"])
            if jc >= 2:
                te.wait_ge(ss.h, tok[(t, "z_evac", jc - 2)])
                te.wait_ge(sv.h, tok[(t, "zy", jc - 2)])
                te.wait_ge(ss.h, tok[(t, "y_evac", jc - 2)])
            elif t > 0:
                te.wait_ge(sv.h, tok[(t - 1, "dh_done")])
            for dc in range(8):
                mm = te.matmul(pY[:], e["tv_sb"][:, dc * D + jc * 128: dc * D + jc * 128 + 128],
                               xn_fm[:, dc * RS:(dc + 1) * RS],
                               start=(dc == 0), stop=(dc == 7))
            pe.inc(mm)
            for dc in range(8):
                mm = te.matmul(pZ[:], e["tk_sb"][:, dc * D + jc * 128: dc * D + jc * 128 + 128],
                               xn_fm[:, dc * RS:(dc + 1) * RS],
                               start=(dc == 0), stop=(dc == 7))
            pe.inc(mm)
        tok[(t, "yz", jc)] = pe.v

        @block.scalar
        def _(s, jc=jc, pY=pY, t=t):
            s.wait_ge(pe.h, tok[(t, "yz", jc)] - 1)
            if jc >= 1:
                s.wait_ge(sv.h, tok[(t, "zy", jc - 1)])
            elif t > 0:
                s.wait_ge(sv.h, tok[(t - 1, "zy", 7)])
            ss.inc(s.activation(y_sb[:, 0:RS], pY[:], AF.Copy))
        tok[(t, "y_evac", jc)] = ss.v

        @block.vector
        def _(v, jc=jc, pZ=pZ, t=t):
            v.wait_ge(pe.h, tok[(t, "yz", jc)])
            v.wait_ge(ss.h, tok[(t, "y_evac", jc)])
            if t > 0 and jc == 0:
                v.wait_ge(ts.h, tok[(t - 1, "zb_T")])
                v.wait_ge(sv.h, tok[(t - 1, "dr_done", 7)])
            sv.inc(v.tensor_tensor(zy_bf[:, jc * RS:(jc + 1) * RS], pZ[:],
                                   y_sb[:, 0:RS], A.subtract))
        tok[(t, "zy", jc)] = sv.v

        @block.scalar
        def _(s, jc=jc, pZ=pZ, t=t):
            s.wait_ge(sv.h, tok[(t, "zy", jc)])
            if t > 0 and jc == 0:
                s.wait_ge(pe.h, tok[(t - 1, "hpre", 1)])
            ss.inc(s.activation(z_bf[:, jc * RS:(jc + 1) * RS], pZ[:], AF.Copy))
        tok[(t, "z_evac", jc)] = ss.v

    @block.gpsimd
    def _(g, t=t):
        g.wait_ge(ss.h, tok[(t, "z_evac", 7)])
        g.wait_ge(sp.h, sp.v)
        g.dma_start(out=e["zb_h"].rearrange("(dc p) r -> p dc r", p=128),
                    in_=z_bf.rearrange("p (dc r) -> p dc r", dc=8)).then_inc(sp.h, 16)
    sp.v += 16
    tok[(t, "zb_sp")] = sp.v

    @block.scalar
    def _(s, t=t):
        s.wait_ge(sp.h, tok[(t, "zb_sp")])
        if t > 0:
            s.wait_ge(pe.h, tok[(t - 1, "g_done")])
        s.wait_ge(ts.h, ts.v)
        s.dma_start_transpose(e["z_tm"].rearrange("p (rt d) -> p rt d", rt=4),
                              e["zb_h"][:, :]).then_inc(ts.h, 16)
    ts.v += 16
    tok[(t, "zb_T")] = ts.v

    for jc in range(8):
        pX = pC if jc % 2 == 0 else pD

        @block.tensor
        def _(te, jc=jc, pX=pX, t=t):
            if jc >= 2:
                te.wait_ge(ss.h, tok[(t, "qfm_evac", jc - 2)])
            else:
                te.wait_ge(sv.h, tok[(t, "zy", 7)])
                te.wait_ge(ss.h, tok[(t, "z_evac", 7)])
            for dc in range(8):
                mm = te.matmul(pX[:], e["tq_sb"][:, dc * D + jc * 128: dc * D + jc * 128 + 128],
                               xn_fm[:, dc * RS:(dc + 1) * RS],
                               start=(dc == 0), stop=(dc == 7))
            pe.inc(mm)
        tok[(t, "qfm", jc)] = pe.v

        @block.scalar
        def _(s, jc=jc, pX=pX, t=t):
            s.wait_ge(pe.h, tok[(t, "qfm", jc)])
            if t >= 2 and jc == 0:
                s.wait_ge(pe.h, tok[(t - 2, "h2_done")])
            ss.inc(s.activation(q_bf[:, jc * RS:(jc + 1) * RS], pX[:], AF.Copy))
        tok[(t, "qfm_evac", jc)] = ss.v


def emit_step(e, t):
    block, tok = e["block"], e["tok"]
    (ld, ts, sp, sv, ss, pe, cc) = (e[k] for k in ["ld", "ts", "sp", "sv", "ss", "pe", "cc"])
    xn_fm = e["xn_fm"][t % 2]
    q_bf = e["q_bf"][t % 2]
    (z_bf, zy_bf) = (e["z_bf"], e["zy_bf"])
    (w1t_m, w2_m, w2t_m, w1t_b, w2_b, w2t_b) = (
        e[k] for k in ["w1t_m", "w2_m", "w2t_m", "w1t_b", "w2_b", "w2t_b"])
    (b1_m, b2_m, b2row, ones1) = (e["b1_m"], e["b2_m"], e["b2row"], e["ones1"])
    (hpre, h_f, h_bf, dr_bf, dh_bf, h2_bf) = (
        e[k] for k in ["hpre", "h_f", "h_bf", "dr_bf", "dh_bf", "h2_bf"])
    (z_tm, dr_tm, h_tm, dh_tm) = (e["z_tm"], e["dr_tm"], e["h_tm"], e["dh_tm"])
    (scrA, scrB, scrC, gsend, s_sb) = (
        e[k] for k in ["scrA", "scrB", "scrC", "gsend", "s_sb"])
    (pA, pB, pC, pD, pG) = (e["pA"], e["pB"], e["pC"], e["pD"], e["pG"])
    r0 = RS * t

    for cc2 in range(2):
        pX = pA if cc2 == 0 else pB

        @block.tensor
        def _(te, cc2=cc2, pX=pX, t=t):
            te.wait_ge(ss.h, tok[(t, "z_evac", 7)])
            te.wait_ge(sv.h, tok[(t, "zy", 7)])
            if cc2 == 0:
                te.wait_ge(sv.h, e["pre_sv"] if t == 0 else tok[(t - 1, "upd_done")])
                if t > 0:
                    te.wait_ge(ss.h, tok[(t - 1, "h2_evac", 1)])
            for jc in range(8):
                mm = te.matmul(pX[:], w1t_b[:, jc * C + cc2 * 128: jc * C + cc2 * 128 + 128],
                               z_bf[:, jc * RS:(jc + 1) * RS],
                               start=(jc == 0), stop=(jc == 7))
            pe.inc(mm)
        tok[(t, "hpre", cc2)] = pe.v

        @block.scalar
        def _(s, cc2=cc2, pX=pX, t=t):
            s.wait_ge(pe.h, tok[(t, "hpre", cc2)])
            if t > 0 and cc2 == 0:
                s.wait_ge(sv.h, tok[(t - 1, "dh_done")])
            ss.inc(s.activation(hpre[:, cc2 * RS:(cc2 + 1) * RS], pX[:], AF.Identity,
                                bias=b1_m[:, cc2:cc2 + 1]))
            ss.inc(s.activation(h_f[:, cc2 * RS:(cc2 + 1) * RS], pX[:], AF.Gelu,
                                bias=b1_m[:, cc2:cc2 + 1]))
        tok[(t, "h_evac", cc2)] = ss.v

        @block.vector
        def _(v, cc2=cc2, t=t):
            v.wait_ge(ss.h, tok[(t, "h_evac", cc2)])
            if t > 0 and cc2 == 0:
                v.wait_ge(ts.h, tok[(t - 1, "hb_T")])
            sv.inc(v.tensor_copy(h_bf[:, cc2 * RS:(cc2 + 1) * RS],
                                 h_f[:, cc2 * RS:(cc2 + 1) * RS]))
        tok[(t, "h_bf", cc2)] = sv.v

    @block.gpsimd
    def _(g, t=t):
        g.wait_ge(sv.h, tok[(t, "h_bf", 1)])
        g.wait_ge(sp.h, sp.v)
        g.dma_start(out=e["hb_h"].rearrange("(cc p) r -> p cc r", p=128),
                    in_=h_bf.rearrange("p (cc r) -> p cc r", cc=2)).then_inc(sp.h, 16)
    sp.v += 16
    tok[(t, "hb_sp")] = sp.v

    @block.scalar
    def _(s, t=t):
        s.wait_ge(sp.h, tok[(t, "hb_sp")])
        if t > 0:
            s.wait_ge(pe.h, tok[(t - 1, "g_done")])
        s.wait_ge(ts.h, ts.v)
        s.dma_start_transpose(e["h_tm"].rearrange("p (rt c) -> p rt c", rt=4),
                              e["hb_h"][:, :]).then_inc(ts.h, 16)
    ts.v += 16
    tok[(t, "hb_T")] = ts.v

    for dc in range(8):
        pX = pC if dc % 2 == 0 else pD

        @block.tensor
        def _(te, dc=dc, pX=pX, t=t):
            te.wait_ge(sv.h, tok[(t, "h_bf", 1)])
            if dc < 2:
                te.wait_ge(ss.h, tok[(t, "qfm_evac", 7)])
            else:
                te.wait_ge(sv.h, tok[(t, "dr_done", dc - 2)])
            for cc2 in range(2):
                mm = te.matmul(pX[:], w2t_b[:, cc2 * D + dc * 128: cc2 * D + dc * 128 + 128],
                               h_bf[:, cc2 * RS:(cc2 + 1) * RS],
                               start=(cc2 == 0), stop=(cc2 == 1))
            pe.inc(mm)
        tok[(t, "rmm", dc)] = pe.v

        @block.vector
        def _(v, dc=dc, pX=pX, t=t):
            v.wait_ge(pe.h, tok[(t, "rmm", dc)])
            if t > 0 and dc == 0:
                v.wait_ge(ts.h, tok[(t - 1, "drb_T")])
            sv.inc(v.scalar_tensor_tensor(dr_bf[:, dc * RS:(dc + 1) * RS], pX[:],
                                          b2_m[:, dc:dc + 1],
                                          zy_bf[:, dc * RS:(dc + 1) * RS], A.add, A.add))
            v.wait_ge(sv.h, sv.v)
            with e["nc"].allow_low_precision(reason="bf16 grad-transfer payload"):
                sv.inc(v.reduce_sum(gsend[:, 6146 + dc:6147 + dc],
                                    dr_bf[:, dc * RS:(dc + 1) * RS], axis=AX))
        tok[(t, "dr_done", dc)] = sv.v

    @block.gpsimd
    def _(g, t=t):
        g.wait_ge(sv.h, tok[(t, "dr_done", 7)])
        g.wait_ge(sp.h, sp.v)
        g.dma_start(out=e["drb_h"].rearrange("(dc p) r -> p dc r", p=128),
                    in_=dr_bf.rearrange("p (dc r) -> p dc r", dc=8)).then_inc(sp.h, 16)
    sp.v += 16
    tok[(t, "drb_sp")] = sp.v

    @block.scalar
    def _(s, t=t):
        s.wait_ge(sp.h, tok[(t, "drb_sp")])
        if t > 0:
            s.wait_ge(pe.h, tok[(t - 1, "g_done")])
        s.wait_ge(ts.h, ts.v)
        s.dma_start_transpose(e["dr_tm"].rearrange("p (rt d) -> p rt d", rt=4),
                              e["drb_h"][:, :]).then_inc(ts.h, 16)
    ts.v += 16
    tok[(t, "drb_T")] = ts.v

    for cc2 in range(2):
        pX = pA if cc2 == 0 else pB

        @block.tensor
        def _(te, cc2=cc2, pX=pX, t=t):
            te.wait_ge(sv.h, tok[(t, "dr_done", 7)])
            te.wait_ge(ss.h, tok[(t, "h_evac", cc2)])
            for dc in range(8):
                mm = te.matmul(pX[:], w2_b[:, dc * C + cc2 * 128: dc * C + cc2 * 128 + 128],
                               dr_bf[:, dc * RS:(dc + 1) * RS],
                               start=(dc == 0), stop=(dc == 7))
            pe.inc(mm)
        tok[(t, "dhmm", cc2)] = pe.v

    @block.scalar
    def _(s, t=t):
        s.wait_ge(sv.h, tok[(t, "h_bf", 1)])
        if t > 0:
            s.wait_ge(sp.h, tok[(t - 1, "s_spill", 1)])
        ss.inc(s.activation(scrA[:], hpre[:], AF.Square))
        s.wait_ge(ss.h, ss.v)
        ss.inc(s.activation(scrB[:], scrA[:], AF.Exp, scale=-0.5))
    tok[(t, "ephi")] = ss.v

    @block.vector
    def _(v, t=t):
        def tv(i):
            v.wait_ge(sv.h, sv.inc(i))
        v.wait_ge(ss.h, tok[(t, "ephi")])
        v.wait_ge(pe.h, tok[(t, "dhmm", 1)])
        tv(v.tensor_scalar(scrC[:], hpre[:], 0.0, None, A.is_equal))
        tv(v.tensor_tensor(scrA[:], hpre[:], scrC[:], A.add))
        tv(v.reciprocal(scrA[:], scrA[:]))
        tv(v.tensor_tensor(scrA[:], h_f[:], scrA[:], A.mult))
        tv(v.scalar_tensor_tensor(scrA[:], scrC[:], 0.5, scrA[:], A.mult, A.add))
        tv(v.tensor_tensor(scrB[:], scrB[:], hpre[:], A.mult))
        tv(v.scalar_tensor_tensor(scrB[:], scrB[:], 0.3989422804014327, scrA[:],
                                  A.mult, A.add))
        if t > 0:
            v.wait_ge(ts.h, tok[(t - 1, "dhb_T")])
        for cc2 in range(2):
            pX = pA if cc2 == 0 else pB
            tv(v.tensor_tensor(dh_bf[:, cc2 * RS:(cc2 + 1) * RS], pX[:],
                               scrB[:, cc2 * RS:(cc2 + 1) * RS], A.mult))
            with e["nc"].allow_low_precision(reason="bf16 grad-transfer payload"):
                tv(v.reduce_sum(gsend[:, 6144 + cc2:6145 + cc2],
                                dh_bf[:, cc2 * RS:(cc2 + 1) * RS], axis=AX))
    tok[(t, "dh_done")] = sv.v

    @block.gpsimd
    def _(g, t=t):
        g.wait_ge(sv.h, tok[(t, "dh_done")])
        g.wait_ge(sp.h, sp.v)
        g.dma_start(out=e["dhb_h"].rearrange("(cc p) r -> p cc r", p=128),
                    in_=dh_bf.rearrange("p (cc r) -> p cc r", cc=2)).then_inc(sp.h, 16)
    sp.v += 16
    tok[(t, "dhb_sp")] = sp.v

    @block.scalar
    def _(s, t=t):
        s.wait_ge(sp.h, tok[(t, "dhb_sp")])
        if t > 0:
            s.wait_ge(pe.h, tok[(t - 1, "g_done")])
        s.wait_ge(ts.h, ts.v)
        s.dma_start_transpose(e["dh_tm"].rearrange("p (rt c) -> p rt c", rt=4),
                              e["dhb_h"][:, :]).then_inc(ts.h, 16)
    ts.v += 16
    tok[(t, "dhb_T")] = ts.v

    @block.tensor
    def _(te, t=t):
        te.wait_ge(ts.h, tok[(t, "zb_T")])
        te.wait_ge(ts.h, tok[(t, "dhb_T")])
        if t > 0:
            te.wait_ge(sv.h, tok[(t - 1, "s_evac", 1)])
        for dc in range(8):
            for rc in range(4):
                mm = te.matmul(pG[:, dc * C:(dc + 1) * C],
                               z_tm[:, rc * D + dc * 128: rc * D + dc * 128 + 128],
                               dh_tm[:, rc * C:(rc + 1) * C],
                               start=(rc == 0), stop=(rc == 3))
        pe.inc(mm)
    tok[(t, "gw1t")] = pe.v

    @block.vector
    def _(v, t=t):
        v.wait_ge(pe.h, tok[(t, "gw1t")])
        sv.inc(v.tensor_copy(gsend[:, 0:GW], pG[:]))
    tok[(t, "gw1t_evac")] = sv.v

    @block.tensor
    def _(te, t=t):
        te.wait_ge(ts.h, tok[(t, "drb_T")])
        te.wait_ge(ts.h, tok[(t, "hb_T")])
        te.wait_ge(sv.h, tok[(t, "gw1t_evac")])
        for dc in range(8):
            for rc in range(4):
                mm = te.matmul(pG[:, dc * C:(dc + 1) * C],
                               dr_tm[:, rc * D + dc * 128: rc * D + dc * 128 + 128],
                               h_tm[:, rc * C:(rc + 1) * C],
                               start=(rc == 0), stop=(rc == 3))
        pe.inc(mm)
    tok[(t, "gw2")] = pe.v

    @block.vector
    def _(v, t=t):
        v.wait_ge(pe.h, tok[(t, "gw2")])
        sv.inc(v.tensor_copy(gsend[:, GW:2 * GW], pG[:]))
    tok[(t, "gw2_evac")] = sv.v

    @block.tensor
    def _(te, t=t):
        te.wait_ge(sv.h, tok[(t, "gw2_evac")])
        for cc2 in range(2):
            for dh2 in range(2):
                for rc in range(4):
                    mm = te.matmul(pG[:, cc2 * D + dh2 * 512: cc2 * D + dh2 * 512 + 512],
                                   h_tm[:, rc * C + cc2 * 128: rc * C + cc2 * 128 + 128],
                                   dr_tm[:, rc * D + dh2 * 512: rc * D + dh2 * 512 + 512],
                                   start=(rc == 0), stop=(rc == 3))
        pe.inc(mm)
    tok[(t, "gw2t")] = pe.v
    tok[(t, "g_done")] = pe.v

    @block.vector
    def _(v, t=t):
        v.wait_ge(pe.h, tok[(t, "gw2t")])
        sv.inc(v.tensor_copy(gsend[:, 2 * GW:3 * GW], pG[:]))
    tok[(t, "gw2t_evac")] = sv.v

    @block.gpsimd
    def _(g, t=t):
        g.wait_ge(sv.h, tok[(t, "gw2t_evac")])
        g.wait_ge(sv.h, tok[(t, "dh_done")])
        g.wait_ge(sp.h, sp.v)
        g.dma_start(out=e["ar_in"][:, :], in_=gsend[:]).then_inc(sp.h, 16)
        g.wait_ge(sp.h, sp.v + 16)
        cc.inc(g.collective_compute(
            "AllReduce", A.add, replica_groups=[list(range(NCORES))],
            ins=[e["ar_in"].flatten()[None, :].opt()],
            outs=[e["ar_out"].flatten()[None, :].opt()]))
        g.wait_ge(cc.h, cc.v)
        g.dma_start(out=gsend[:], in_=e["ar_out"][:, :]).then_inc(sp.h, 16)
    sp.v += 32
    tok[(t, "ar_done")] = sp.v

    if t + 1 < B:
        emit_front(e, t + 1)

    @block.vector
    def _(v, t=t):
        def tv(i):
            v.wait_ge(sv.h, sv.inc(i))
        v.wait_ge(sp.h, tok[(t, "ar_done")])
        tv(v.scalar_tensor_tensor(w1t_m[:], gsend[:, 0:GW], -LR2, w1t_m[:], A.mult, A.add))
        tv(v.scalar_tensor_tensor(b1_m[:], gsend[:, 6144:6146], -LR2, b1_m[:], A.mult, A.add))
        tv(v.tensor_copy(w1t_b[:], w1t_m[:]))
    tok[(t, "upd_w1t")] = sv.v

    @block.vector
    def _(v, t=t):
        def tv(i):
            v.wait_ge(sv.h, sv.inc(i))
        tv(v.scalar_tensor_tensor(w2_m[:], gsend[:, GW:2 * GW], -LR2, w2_m[:], A.mult, A.add))
        tv(v.tensor_copy(w2_b[:], w2_m[:]))
        tv(v.scalar_tensor_tensor(w2t_m[:], gsend[:, 2 * GW:3 * GW], -LR2, w2t_m[:],
                                  A.mult, A.add))
        tv(v.tensor_copy(w2t_b[:], w2t_m[:]))
        tv(v.scalar_tensor_tensor(b2_m[:], gsend[:, 6146:6154], -LR2, b2_m[:], A.mult, A.add))
    tok[(t, "upd_done")] = sv.v

    @block.sync
    def _(sy, t=t):
        sy.wait_ge(sv.h, tok[(t, "upd_done")])
        if t > 0:
            sy.wait_ge(pe.h, tok[(t - 1, "xn_consumed")])
        sy.wait_ge(ld.h, ld.v)
        sy.dma_start(out=e["b2r_h"].rearrange("(dc p) -> p dc", p=128), in_=b2_m[:],
                     allow_slow_non_contiguous=True).then_inc(ld.h, 16)
        sy.wait_ge(ld.h, ld.v + 16)
        sy.dma_start(out=b2row[0:1, :], in_=e["b2r_h"][None, :]).then_inc(ld.h, 16)
    ld.v += 32
    tok[(t, "b2row_done_ld")] = ld.v

    for cc2 in range(2):
        pX = pA if cc2 == 0 else pB

        @block.tensor
        def _(te, cc2=cc2, pX=pX, t=t):
            te.wait_ge(sv.h, tok[(t, "upd_w1t")])
            te.wait_ge(ss.h, tok[(t, "qfm_evac", 7)])
            te.wait_ge(sv.h, tok[(t, "dh_done")])
            if t + 1 < B and cc2 == 0:
                te.wait_ge(ss.h, tok[(t + 1, "z_evac", 7)])
                te.wait_ge(sv.h, tok[(t + 1, "zy", 7)])
                te.wait_ge(ss.h, tok[(t + 1, "qfm_evac", 7)])
            for jc in range(8):
                mm = te.matmul(pX[:], w1t_b[:, jc * C + cc2 * 128: jc * C + cc2 * 128 + 128],
                               q_bf[:, jc * RS:(jc + 1) * RS],
                               start=(jc == 0), stop=(jc == 7))
            pe.inc(mm)
        tok[(t, "h2mm", cc2)] = pe.v

        @block.scalar
        def _(s, cc2=cc2, pX=pX, t=t):
            s.wait_ge(pe.h, tok[(t, "h2mm", cc2)])
            if t > 0 and cc2 == 0:
                s.wait_ge(pe.h, tok[(t - 1, "xn_consumed")])
            ss.inc(s.activation(h2_bf[:, cc2 * RS:(cc2 + 1) * RS], pX[:], AF.Gelu,
                                bias=b1_m[:, cc2:cc2 + 1]))
        tok[(t, "h2_evac", cc2)] = ss.v
    tok[(t, "h2_done")] = pe.v

    for half in range(2):
        for rc in (2 * half, 2 * half + 1):
            slot = rc % 2

            @block.tensor
            def _(te, rc=rc, slot=slot, half=half, t=t):
                te.wait_ge(ss.h, tok[(t, "h2_evac", 1)])
                te.wait_ge(sv.h, tok[(t, "upd_done")])
                te.wait_ge(sv.h, tok[(t, "gw2t_evac")])
                te.wait_ge(ld.h, tok[(t, "b2row_done_ld")])
                if half == 1 and rc == 2:
                    te.wait_ge(sv.h, tok[(t, "s_evac", 0)])
                for jh in range(2):
                    reg = pG[:, slot * D + jh * 512: slot * D + jh * 512 + 512]
                    for dc in range(8):
                        te.matmul(reg,
                                  xn_fm[:, dc * RS + rc * 128: dc * RS + rc * 128 + 128],
                                  e["tq_sb"][:, dc * D + jh * 512: dc * D + jh * 512 + 512],
                                  start=(dc == 0), stop=False)
                    for cc2 in range(2):
                        te.matmul(reg,
                                  h2_bf[:, cc2 * RS + rc * 128: cc2 * RS + rc * 128 + 128],
                                  w2t_b[:, cc2 * D + jh * 512: cc2 * D + jh * 512 + 512],
                                  start=False, stop=False)
                    mm = te.matmul(reg, ones1[:], b2row[0:1, jh * 512: jh * 512 + 512],
                                   start=False, stop=True)
                pe.inc(mm)
            tok[(t, "s_mm", rc)] = pe.v
        if half == 1:
            tok[(t, "xn_consumed")] = pe.v

        @block.vector
        def _(v, half=half, t=t):
            for rc in (2 * half, 2 * half + 1):
                slot = rc % 2
                v.wait_ge(pe.h, tok[(t, "s_mm", rc)])
                if half == 0 and t > 0:
                    v.wait_ge(sp.h, tok[(t - 1, "s_spill", 1)])
                if half == 1:
                    v.wait_ge(sp.h, tok[(t, "s_spill", 0)])
                sv.inc(v.tensor_copy(s_sb[:, slot * D:(slot + 1) * D],
                                     pG[:, slot * D:(slot + 1) * D]))
        tok[(t, "s_evac", half)] = sv.v

        @block.gpsimd
        def _(g, half=half, t=t):
            g.wait_ge(sv.h, tok[(t, "s_evac", half)])
            for i, rc in enumerate((2 * half, 2 * half + 1)):
                slot = rc % 2
                g.wait_ge(sp.h, sp.v + 16 * i)
                g.dma_start(out=e["s_hbm"][r0 + rc * 128: r0 + (rc + 1) * 128, :],
                            in_=s_sb[:, slot * D:(slot + 1) * D]).then_inc(sp.h, 16)
        sp.v += 32
        tok[(t, "s_spill", half)] = sp.v



NB5 = (NI + 511) // 512


def build_s4(fused=None):
    P = fused or Prog("s4")
    nc = P.nc
    xq_hbm = P.inp("xq_bf", (B * RS, D), BF16)
    wg_h = P.inp("wqg_full", (NI, D), BF16)
    wu_h = P.inp("wqu_full", (NI, D), BF16)
    hq_hbm = P.out("hq_bf", (B * RS, NI), BF16)

    wgT = P.sb("s4_wgT", [128, 8 * NI], BF16)
    wuT = P.sb("s4_wuT", [128, 8 * NI], BF16)
    xq_fm = [P.sb(f"s4_xqfm{i}", [128, 8 * RS], BF16) for i in range(2)]
    gs = P.sb("s4_gs", [128, NI], F32)
    hf = [P.sb(f"s4_hf{i}", [128, NI], F32) for i in range(3)]
    hqb = [P.sb(f"s4_hqb{i}", [128, NI], BF16) for i in range(2)]
    HW4 = NI // 2
    scrq = [quant_scratch3(P, HW4, "s4a"), quant_scratch3(P, HW4, "s4b")]
    pp = [P.ps(f"s4_p{i}", [128, 512]) for i in range(8)]

    ts = P.sem("s4_ts")
    sv = P.sem("s4_sv")
    gv = P.sem("s4_gv")
    ss = P.sem("s4_ss")
    pe = P.sem("s4_pe")
    sp = P.sem("s4_sp")

    tok = {}
    pc = [0]
    pfree = {}
    block = P.get_block()

    @block.scalar
    def _(s):
        kk = [0]
        for dst, srch in [(wgT, wg_h), (wuT, wu_h)]:
            for k in range(4):
                n0 = k * (NI // 4)
                s.wait_ge(ts.h, ts.v + 16 * kk[0]); kk[0] += 1
                s.dma_start_transpose(
                    dst.rearrange("p (dc n) -> p dc n", dc=8)[:, :, n0:n0 + NI // 4],
                    srch[n0:n0 + NI // 4, :]).then_inc(ts.h, 16)
    ts.v += 8 * 16
    w_tok = ts.v

    for ch in range(8):
        cpar = ch % 2

        @block.scalar
        def _(s, ch=ch, cpar=cpar):
            if ch >= 2:
                s.wait_ge(pe.h, tok[("xq_used", ch - 2)])
            s.wait_ge(ts.h, ts.v)
            s.dma_start_transpose(xq_fm[cpar].rearrange("p (dc r) -> p dc r", dc=8),
                                  xq_hbm[512 * ch:512 * (ch + 1), :]).then_inc(ts.h, 16)
        ts.v += 16
        tok[("xq_fm", ch)] = ts.v

        for tt in range(4):
            t_idx = 4 * ch + tt
            tpar = t_idx % 3
            hpar = t_idx % 2

            phases = [(0, (0, 1)), (1, (2, 3)), (2, (4, 5))]
            gpair = {}
            upair = {}
            for ph, nbs in phases:
                pr = pc[0] % 4
                pc[0] += 1
                gpair[ph] = pr

                @block.tensor
                def _(te, ch=ch, tt=tt, cpar=cpar, t_idx=t_idx, ph=ph, nbs=nbs, pr=pr):
                    te.wait_ge(ts.h, tok[("xq_fm", ch)])
                    if pr in pfree:
                        sem, val = pfree[pr]
                        te.wait_ge(sem.h, val)
                    for dc in range(8):
                        for k, nb in enumerate(nbs):
                            w = min(512, NI - nb * 512)
                            mm = te.matmul(pp[2 * pr + k][:, 0:w],
                                           xq_fm[cpar][:, dc * RS + tt * 128: dc * RS + tt * 128 + 128],
                                           wgT[:, dc * NI + nb * 512: dc * NI + nb * 512 + w],
                                           start=(dc == 0), stop=(dc == 7))
                    pe.inc(mm)
                tok[("gate", t_idx, ph)] = pe.v

                @block.scalar
                def _(s, t_idx=t_idx, ph=ph, nbs=nbs, pr=pr):
                    s.wait_ge(pe.h, tok[("gate", t_idx, ph)])
                    if ("hmul", t_idx - 1, ph) in tok:
                        s.wait_ge(sv.h, tok[("hmul", t_idx - 1, ph)])
                    for k, nb in enumerate(nbs):
                        w = min(512, NI - nb * 512)
                        ss.inc(s.activation(gs[:, nb * 512: nb * 512 + w],
                                            pp[2 * pr + k][:, 0:w], AF.Silu))
                tok[("gs", t_idx, ph)] = ss.v
                pfree[pr] = (ss, ss.v)

            for ph, nbs in phases:
                pr = pc[0] % 4
                pc[0] += 1
                upair[ph] = pr

                @block.tensor
                def _(te, ch=ch, tt=tt, cpar=cpar, t_idx=t_idx, ph=ph, nbs=nbs, pr=pr):
                    if pr in pfree:
                        sem, val = pfree[pr]
                        te.wait_ge(sem.h, val)
                    for dc in range(8):
                        for k, nb in enumerate(nbs):
                            w = min(512, NI - nb * 512)
                            mm = te.matmul(pp[2 * pr + k][:, 0:w],
                                           xq_fm[cpar][:, dc * RS + tt * 128: dc * RS + tt * 128 + 128],
                                           wuT[:, dc * NI + nb * 512: dc * NI + nb * 512 + w],
                                           start=(dc == 0), stop=(dc == 7))
                    pe.inc(mm)
                tok[("up", t_idx, ph)] = pe.v

                @block.vector
                def _(v, t_idx=t_idx, tpar=tpar, ph=ph, nbs=nbs, pr=pr):
                    v.wait_ge(pe.h, tok[("up", t_idx, ph)])
                    v.wait_ge(ss.h, tok[("gs", t_idx, ph)])
                    if ph == 0 and ("hq_spill", t_idx - 3) in tok:
                        v.wait_ge(sp.h, tok[("hq_spill", t_idx - 3)])
                    for k, nb in enumerate(nbs):
                        w = min(512, NI - nb * 512)
                        sv.inc(v.tensor_tensor(hf[tpar][:, nb * 512: nb * 512 + w],
                                               pp[2 * pr + k][:, 0:w],
                                               gs[:, nb * 512: nb * 512 + w], A.mult))
                    v.wait_ge(sv.h, sv.v)
                tok[("hmul", t_idx, ph)] = sv.v
                pfree[pr] = (sv, sv.v)
            tok[("h_done", t_idx)] = sv.v

            tok[("xq_used", ch)] = pe.v
            qsems = {"sv": sv, "ss": ss, "gv": gv}

            def ogd(g, t_idx=t_idx):
                if ("hq_spill", t_idx - 2) in tok:
                    g.wait_ge(sp.h, tok[("hq_spill", t_idx - 2)])

            def emit_spill(tile, par):
                @block.gpsimd
                def _(g, tile=tile, par=par):
                    g.wait_ge(gv.h, gv.v)
                    g.wait_ge(sp.h, sp.v)
                    g.dma_start(out=hq_hbm[128 * tile:128 * (tile + 1), :],
                                in_=hqb[par][:]).then_inc(sp.h, 16)
                sp.v += 16
                tok[("hq_spill", tile)] = sp.v

            q3_pre(block, qsems, scrq[0], hf[tpar][:, 0:HW4], HW4)
            if "q_pend" in tok:
                pt, ppar = tok.pop("q_pend")
                q3_ladder(block, qsems, scrq[1], hqb[ppar][:, HW4:2 * HW4], HW4,
                          out_guard=ogd)
                emit_spill(pt, ppar)
            q3_pre(block, qsems, scrq[1], hf[tpar][:, HW4:2 * HW4], HW4)
            q3_ladder(block, qsems, scrq[0], hqb[hpar][:, 0:HW4], HW4, out_guard=ogd)
            tok["q_pend"] = (t_idx, hpar)

    if "q_pend" in tok:
        pt, ppar = tok.pop("q_pend")

        def ogd2(g, pt=pt):
            if ("hq_spill", pt - 2) in tok:
                g.wait_ge(sp.h, tok[("hq_spill", pt - 2)])
        q3_ladder(block, {"sv": sv, "ss": ss, "gv": gv}, scrq[1],
                  hqb[ppar][:, HW4:2 * HW4], HW4, out_guard=ogd2)

        @block.gpsimd
        def _(g, pt=pt, ppar=ppar):
            g.wait_ge(gv.h, gv.v)
            g.wait_ge(sp.h, sp.v)
            g.dma_start(out=hq_hbm[128 * pt:128 * (pt + 1), :],
                        in_=hqb[ppar][:]).then_inc(sp.h, 16)
        sp.v += 16

    @block.gpsimd
    def _(g):
        g.wait_ge(sp.h, sp.v)

    return P



def build_s5(fused=None):
    P = fused or Prog("s5")
    nc = P.nc
    hq_hbm = P.inp("hq_bf", (B * RS, NI), BF16)
    wd_h = P.inp("wqd_full", (D, NI), BF16)
    s_hbm = P.inp("s_f32", (B * RS, D), F32)
    x_hbm = P.inp("x_shard2", (B * RS, D), F32)
    out_hbm = P.out("out_shard", (B * RS, D), F32)

    wdT = P.sb("s5_wdT", [128, 22 * D], BF16)
    hq_fm = [P.sb(f"s5_hqfm{i}", [128, 22 * RS], BF16) for i in range(2)]
    x_sb = [P.sb(f"s5_x{i}", [128, D], F32) for i in range(2)]
    s_sb = [P.sb(f"s5_s{i}", [128, D], F32) for i in range(2)]
    g_sb = [P.sb(f"s5_g{i}", [128, D], F32) for i in range(2)]
    o_sb = [P.sb(f"s5_o{i}", [128, D], F32) for i in range(2)]
    pp = [P.ps(f"s5_p{i}", [128, 512]) for i in range(8)]

    ld = P.sem("s5_ld")
    ts = P.sem("s5_ts")
    sv = P.sem("s5_sv")
    ss = P.sem("s5_ss")
    pe = P.sem("s5_pe")
    sp = P.sem("s5_sp")

    tok = {}
    block = P.get_block()

    @block.scalar
    def _(s):
        for k in range(2):
            d0 = k * 512
            s.wait_ge(ts.h, ts.v + 16 * k)
            s.dma_start_transpose(
                wdT.rearrange("p (nc d) -> p nc d", nc=22)[:, :, d0:d0 + 512],
                wd_h[d0:d0 + 512, :]).then_inc(ts.h, 16)
    ts.v += 32
    w_tok = ts.v

    for ch in range(8):
        cpar = ch % 2

        @block.scalar
        def _(s, ch=ch, cpar=cpar):
            if ch >= 2:
                s.wait_ge(pe.h, tok[("hq_used", ch - 2)])
            s.wait_ge(ts.h, ts.v)
            s.dma_start_transpose(hq_fm[cpar].rearrange("p (nc r) -> p nc r", nc=22),
                                  hq_hbm[512 * ch:512 * (ch + 1), :]).then_inc(ts.h, 16)
        ts.v += 16
        tok[("hq_fm", ch)] = ts.v

        for tt in range(4):
            tpar = tt % 2
            qpar = (4 * ch + tt) % 4
            t_idx = 4 * ch + tt

            @block.sync
            def _(sy, t_idx=t_idx, tpar=tpar):
                if ("out_done", t_idx - 2) in tok:
                    sy.wait_ge(sv.h, tok[("out_done", t_idx - 2)])
                sy.wait_ge(ld.h, ld.v)
                sy.dma_start(out=x_sb[tpar][:],
                             in_=x_hbm[128 * t_idx:128 * (t_idx + 1), :]).then_inc(ld.h, 16)
                sy.wait_ge(ld.h, ld.v + 16)
                sy.dma_start(out=s_sb[tpar][:],
                             in_=s_hbm[128 * t_idx:128 * (t_idx + 1), :]).then_inc(ld.h, 16)
            ld.v += 32
            tok[("xs_ld", t_idx)] = ld.v

            @block.tensor
            def _(te, ch=ch, tt=tt, cpar=cpar, t_idx=t_idx, qpar=qpar):
                te.wait_ge(ts.h, tok[("hq_fm", ch)])
                if ("g_evac", t_idx - 3) in tok:
                    te.wait_ge(ss.h, tok[("g_evac", t_idx - 3)])
                for nc2 in range(22):
                    for dh2 in range(2):
                        mm = te.matmul(
                            pp[2 * qpar + dh2][:],
                            hq_fm[cpar][:, nc2 * RS + tt * 128: nc2 * RS + tt * 128 + 128],
                            wdT[:, nc2 * D + dh2 * 512: nc2 * D + dh2 * 512 + 512],
                            start=(nc2 == 0), stop=(nc2 == 21))
                pe.inc(mm)
            tok[("down", t_idx)] = pe.v
            tok[("hq_used", ch)] = pe.v

            @block.scalar
            def _(s, t_idx=t_idx, tpar=tpar, qpar=qpar):
                s.wait_ge(pe.h, tok[("down", t_idx)])
                if ("out_done", t_idx - 2) in tok:
                    s.wait_ge(sv.h, tok[("out_done", t_idx - 2)])
                for dh2 in range(2):
                    ss.inc(s.activation(g_sb[tpar][:, dh2 * 512:(dh2 + 1) * 512],
                                        pp[2 * qpar + dh2][:], AF.Silu))
            tok[("g_evac", t_idx)] = ss.v

            @block.vector
            def _(v, t_idx=t_idx, tpar=tpar):
                v.wait_ge(ss.h, tok[("g_evac", t_idx)])
                v.wait_ge(ld.h, tok[("xs_ld", t_idx)])
                if ("out_spill", t_idx - 2) in tok:
                    v.wait_ge(sp.h, tok[("out_spill", t_idx - 2)])
                sv.inc(v.tensor_tensor(o_sb[tpar][:], s_sb[tpar][:], g_sb[tpar][:], A.mult))
                v.wait_ge(sv.h, sv.v)
                sv.inc(v.tensor_tensor(o_sb[tpar][:], o_sb[tpar][:], x_sb[tpar][:], A.add))
            tok[("out_done", t_idx)] = sv.v

            @block.gpsimd
            def _(g, t_idx=t_idx, tpar=tpar):
                g.wait_ge(sv.h, tok[("out_done", t_idx)])
                g.wait_ge(sp.h, sp.v)
                g.dma_start(out=out_hbm[128 * t_idx:128 * (t_idx + 1), :],
                            in_=o_sb[tpar][:]).then_inc(sp.h, 16)
            sp.v += 16
            tok[("out_spill", t_idx)] = sp.v

    @block.gpsimd
    def _(g):
        g.wait_ge(sp.h, sp.v)

    return P
